# version 01
# baseline (speedup 1.0000x reference)
import math
from contextlib import ExitStack

import numpy as np
import ml_dtypes

import concourse.bass as bass
import concourse.mybir as mybir
from concourse.bass_utils import run_bass_kernel_spmd

F32 = mybir.dt.float32
BF16 = mybir.dt.bfloat16
I32 = mybir.dt.int32
U8 = mybir.dt.uint8
AF = mybir.ActivationFunctionType
ALU = mybir.AluOpType
AX = mybir.AxisListType
BF = ml_dtypes.bfloat16

NCORES = 8
BASS_KW = {}
B, L, H = 2, 4096, 4096
DIN, NST, KC, RR = 8192, 16, 4, 256
T = B * L
TPC = T // NCORES
DPC = DIN // NCORES
NH, NKV, HD = 32, 8, 128
EPS_LN = 1e-5
EPS_RMS = 1e-6
NEGBIG = -1.0e30

IDENT_F32 = np.eye(128, dtype=np.float32)
IDENT_BF16 = np.eye(128, dtype=BF)


class Sem:

    def __init__(self, nc, ctx, name):
        self.h = ctx.enter_context(nc.semaphore(name))
        self.n = 0

    def inc(self, instr, k=1):
        instr.then_inc(self.h, k)
        self.n += k
        return instr



def build_ln_t():
    nc = bass.Bass("TRN2", target_bir_lowering=False, **BASS_KW)
    x = nc.dram_tensor("x", [TPC, H], F32, kind="ExternalInput").ap()
    cst = nc.dram_tensor("cst", [128, 4], F32, kind="ExternalInput").ap()
    idf = nc.dram_tensor("idf", [128, 128], F32, kind="ExternalInput").ap()
    idb = nc.dram_tensor("idb", [128, 128], BF16, kind="ExternalInput").ap()
    xlnT = nc.dram_tensor("xlnT", [H, TPC], BF16, kind="ExternalOutput").ap()
    xT = nc.dram_tensor("xT", [H, TPC], F32, kind="ExternalOutput").ap()
    xlnT_v = xlnT.rearrange("(ht p) t -> p ht t", p=128)
    xT_v = xT.rearrange("(ht p) t -> p ht t", p=128)
    NT = TPC // 128
    HT = H // 128

    with ExitStack() as ctx:
        sx = [ctx.enter_context(nc.sbuf_tensor(f"sx{i}", [128, H], F32)) for i in range(2)]
        sq = ctx.enter_context(nc.sbuf_tensor("sq", [128, H], F32))
        xlnb = ctx.enter_context(nc.sbuf_tensor("xlnb", [128, H], BF16))
        sidf = ctx.enter_context(nc.sbuf_tensor("sidf", [128, 128], F32))
        sidb = ctx.enter_context(nc.sbuf_tensor("sidb", [128, 128], BF16))
        scst = ctx.enter_context(nc.sbuf_tensor("scst", [128, 4], F32))
        statV = ctx.enter_context(nc.sbuf_tensor("statV", [128, 8], F32))
        statS = ctx.enter_context(nc.sbuf_tensor("statS", [128, 4], F32))
        vfl = ctx.enter_context(nc.sbuf_tensor("vfl", [128, 512], F32))
        sfl = ctx.enter_context(nc.sbuf_tensor("sfl", [128, 512], F32))
        stgb = [ctx.enter_context(nc.sbuf_tensor(f"stgb{i}", [128, HT * 128], BF16)) for i in range(2)]
        stgf = [ctx.enter_context(nc.sbuf_tensor(f"stgf{i}", [128, HT * 128], F32)) for i in range(2)]
        psb = [ctx.enter_context(nc.psum_tensor(f"psb{i}", [128, 128], BF16)) for i in range(2)]
        psf = [ctx.enter_context(nc.psum_tensor(f"psf{i}", [128, 128], F32)) for i in range(2)]
        ld = [Sem(nc, ctx, "ld0"), Sem(nc, ctx, "ld1")]
        ldc = Sem(nc, ctx, "ldc")
        vv = Sem(nc, ctx, "vv")
        ss = Sem(nc, ctx, "ss")
        tt = Sem(nc, ctx, "tt")
        ee = Sem(nc, ctx, "ee")
        st = Sem(nc, ctx, "st")
        block = ctx.enter_context(nc.Block())

        def vflush(vector):
            vector.tensor_copy(vfl[:, 256:512], vfl[:, 0:256])

        def sflush(scalar):
            scalar.copy(sfl[:, 256:512], sfl[:, 0:256])

        lds = []
        VV = 2
        SS = 2
        TT = 2 * HT
        EE = HT
        STI = 32

        @block.sync
        def _(sync):
            ldc.inc(sync.dma_start(out=sidf[:], in_=idf), 16)
            ldc.inc(sync.dma_start(out=sidb[:], in_=idb), 16)
            ldc.inc(sync.dma_start(out=scst[:], in_=cst), 16)
            for i in range(NT):
                if i >= 2:
                    sync.wait_ge(tt.h, (i - 1) * TT)
                ld[i % 2].inc(sync.dma_start(out=sx[i % 2][:], in_=x[i * 128:(i + 1) * 128, :]), 16)
                lds.append(ld[i % 2].n)

        @block.vector
        def _(vector):
            vector.memset(vfl[:], 0.0)
            for i in range(NT):
                xt = sx[i % 2]
                vector.wait_ge(ld[i % 2].h, lds[i])
                vector.tensor_reduce(statV[:, 0:1], xt[:], AX.X, ALU.add)
                vector.wait_ge(ss.h, SS * i + 1)
                vector.tensor_reduce(statV[:, 1:2], sq[:], AX.X, ALU.add)
                vflush(vector)
                vector.tensor_scalar(statV[:, 2:3], statV[:, 0:1], 1.0 / H, None, ALU.mult)
                vector.tensor_scalar(statV[:, 3:4], statV[:, 1:2], 1.0 / H, None, ALU.mult)
                vflush(vector)
                vector.scalar_tensor_tensor(statV[:, 4:5], statV[:, 2:3], statV[:, 2:3],
                                            statV[:, 3:4], ALU.mult, ALU.subtract)
                vflush(vector)
                vv.inc(vector.tensor_copy(statV[:, 7:8], statV[:, 4:5]))
                vector.wait_ge(ss.h, SS * i + 2)
                if i >= 1:
                    vector.wait_ge(tt.h, i * TT - 1)
                vector.tensor_scalar(xlnb[:], xt[:], statV[:, 2:3], statS[:, 0:1],
                                     ALU.subtract, ALU.mult)
                vflush(vector)
                vv.inc(vector.tensor_copy(statV[:, 7:8], xlnb[:, 0:1]))

        @block.scalar
        def _(scalar):
            scalar.memzero(sfl[:])
            for i in range(NT):
                xt = sx[i % 2]
                scalar.wait_ge(ld[i % 2].h, lds[i])
                if i == 0:
                    scalar.wait_ge(ldc.h, 48)
                if i > 0:
                    scalar.wait_ge(vv.h, VV * (i - 1) + 2)
                scalar.activation(sq[:], xt[:], AF.Square)
                sflush(scalar)
                ss.inc(scalar.copy(statS[:, 1:2], sq[:, 0:1]))
                scalar.wait_ge(vv.h, VV * i + 1)
                scalar.activation(statS[:, 0:1], statV[:, 4:5], AF.Ln,
                                  bias=scst[:, 0:1], scale=-1.0)
                sflush(scalar)
                scalar.activation(statS[:, 0:1], statS[:, 0:1], AF.Exp, scale=-0.5)
                sflush(scalar)
                ss.inc(scalar.copy(statS[:, 2:3], statS[:, 0:1]))
                if i >= 2:
                    scalar.wait_ge(st.h, STI * (i - 1))
                for ht in range(HT):
                    scalar.wait_ge(tt.h, i * TT + 2 * ht + 1)
                    scalar.copy(stgb[i % 2][:, ht * 128:(ht + 1) * 128], psb[ht % 2][:])
                    scalar.wait_ge(tt.h, i * TT + 2 * ht + 2)
                    cp = scalar.copy(stgf[i % 2][:, ht * 128:(ht + 1) * 128], psf[ht % 2][:])
                    if ht < HT - 1:
                        ee.inc(cp)
                sflush(scalar)
                ee.inc(scalar.copy(statS[:, 3:4], sfl[:, 0:1]))

        @block.tensor
        def _(tensor):
            tensor.wait_ge(ldc.h, 48)
            for i in range(NT):
                xt = sx[i % 2]
                tensor.wait_ge(vv.h, VV * i + 2)
                for ht in range(HT):
                    w = i * EE + ht - 1
                    if w > 0:
                        tensor.wait_ge(ee.h, w)
                    tt.inc(tensor.transpose(psb[ht % 2][:], xlnb[:, ht * 128:(ht + 1) * 128], sidb[:]))
                    tt.inc(tensor.transpose(psf[ht % 2][:], xt[:, ht * 128:(ht + 1) * 128], sidf[:]))

        @block.gpsimd
        def _(gpsimd):
            for i in range(NT):
                gpsimd.wait_ge(ee.h, (i + 1) * EE)
                st.inc(gpsimd.dma_start(
                    out=xlnT_v[:, :, i * 128:(i + 1) * 128],
                    in_=stgb[i % 2][:].rearrange("p (ht t) -> p ht t", t=128)), 16)
                st.inc(gpsimd.dma_start(
                    out=xT_v[:, :, i * 128:(i + 1) * 128],
                    in_=stgf[i % 2][:].rearrange("p (ht t) -> p ht t", t=128)), 16)
            gpsimd.wait_ge(st.h, st.n)

    return nc



def build_inproj_conv(debug=False):
    nc = bass.Bass("TRN2", target_bir_lowering=False, **BASS_KW)
    xlnT = nc.dram_tensor("xlnT", [H, T], BF16, kind="ExternalInput").ap()
    Wx = nc.dram_tensor("Wx", [H, DPC], F32, kind="ExternalInput").ap()
    Wz = nc.dram_tensor("Wz", [H, DPC], F32, kind="ExternalInput").ap()
    xprojT = nc.dram_tensor("xprojT", [DPC, 288], F32, kind="ExternalInput").ap()
    convw = nc.dram_tensor("convw", [DPC, 4], F32, kind="ExternalInput").ap()
    convb = nc.dram_tensor("convb", [128, DPC // 128], F32, kind="ExternalInput").ap()
    xc_o = nc.dram_tensor("xc", [DPC, T], F32, kind="ExternalOutput").ap()
    z_o = nc.dram_tensor("z", [DPC, T], F32, kind="ExternalOutput").ap()
    ssmp_o = nc.dram_tensor("ssmp", [T, 288], F32, kind="ExternalOutput").ap()
    wbd = nc.dram_tensor("wbd", [128, 32 * DPC], BF16, kind="ExternalOutput").ap() if debug else None

    xlnT_v = xlnT.rearrange("(kt p) t -> p kt t", p=128)
    xc_v = xc_o.rearrange("(md p) t -> p md t", p=128)
    z_v = z_o.rearrange("(md p) t -> p md t", p=128)
    ssmp_v = ssmp_o.rearrange("(nt p) e -> p nt e", p=128)
    Wx_v = Wx.rearrange("(kt p) d -> p kt d", p=128)
    Wz_v = Wz.rearrange("(kt p) d -> p kt d", p=128)
    xprojT_v = xprojT.rearrange("(md p) e -> p md e", p=128)
    convw_v = convw.rearrange("(md p) k -> p md k", p=128)

    CH = 512
    NCH = T // CH
    KT = H // 128
    MD = DPC // 128
    SLOT = 516

    with ExitStack() as ctx:
        wb = ctx.enter_context(nc.sbuf_tensor("wb", [128, KT * DPC], BF16))
        wtmp = [ctx.enter_context(nc.sbuf_tensor(f"wtmp{i}", [128, DPC], F32)) for i in range(2)]
        xpj = ctx.enter_context(nc.sbuf_tensor("xpj", [128, MD * 288], BF16))
        cw = ctx.enter_context(nc.sbuf_tensor("cw", [128, MD * 4], F32))
        cb = ctx.enter_context(nc.sbuf_tensor("cb", [128, MD], F32))
        rhs = [ctx.enter_context(nc.sbuf_tensor(f"rhs{i}", [128, KT * CH], BF16)) for i in range(2)]
        xcv = ctx.enter_context(nc.sbuf_tensor("xcv", [128, MD * SLOT], F32))
        xcf = ctx.enter_context(nc.sbuf_tensor("xcf", [128, MD * CH], F32))
        xcb = ctx.enter_context(nc.sbuf_tensor("xcb", [128, MD * CH], BF16))
        sstg = [ctx.enter_context(nc.sbuf_tensor(f"sstg{i}", [128, 4 * 288], F32)) for i in range(2)]
        zstg = ctx.enter_context(nc.sbuf_tensor("zstg0", [128, MD * CH], F32))
        sfl = ctx.enter_context(nc.sbuf_tensor("sfl", [128, 512], F32))
        sinkV = ctx.enter_context(nc.sbuf_tensor("sinkV", [128, 4], F32))
        sinkS = ctx.enter_context(nc.sbuf_tensor("sinkS", [128, 4], F32))
        ps = [ctx.enter_context(nc.psum_tensor(f"ps{i}", [128, CH], F32)) for i in range(2)]
        ps2 = [ctx.enter_context(nc.psum_tensor(f"ps2{i}", [128, 288], F32)) for i in range(2)]
        ldc = Sem(nc, ctx, "ldc")
        ldw = [Sem(nc, ctx, "ldw0"), Sem(nc, ctx, "ldw1")]
        ldr = [Sem(nc, ctx, "ldr0"), Sem(nc, ctx, "ldr1")]
        vv = Sem(nc, ctx, "vv")
        ss = Sem(nc, ctx, "ss")
        tt = Sem(nc, ctx, "tt")
        stx = Sem(nc, ctx, "stx")
        stp = [Sem(nc, ctx, "stp0"), Sem(nc, ctx, "stp1")]
        stz = Sem(nc, ctx, "stz")
        block = ctx.enter_context(nc.Block())

        lw_xpj, lw_wx, lw_wz = [], [], []
        lr_a, lr_b = [], []
        VPRE = 36
        VCH = 3
        SS_CH = 14
        TT_CH = 12
        TTA_TOT = TT_CH * NCH
        SSA_TOT = SS_CH * NCH
        ZSS_CH = 9
        ZTT_CH = 8

        @block.sync
        def _(sync):
            ldc.inc(sync.dma_start(out=cw[:], in_=convw_v), 16)
            ldc.inc(sync.dma_start(out=cb[:], in_=convb), 16)
            for j in range(4):
                if j >= 2:
                    sync.wait_ge(vv.h, j - 1)
                p = j % 2
                ldw[p].inc(sync.dma_start(
                    out=wtmp[p][:, 0:576].rearrange("p (a e) -> p a e", e=288),
                    in_=xprojT_v[:, 2 * j:2 * j + 2, :]), 16)
                lw_xpj.append(ldw[p].n)
            for kt in range(KT):
                sync.wait_ge(vv.h, 4 + kt - 1 if kt >= 2 else 4)
                p = kt % 2
                ldw[p].inc(sync.dma_start(out=wtmp[p][:], in_=Wx_v[:, kt, :]), 16)
                lw_wx.append(ldw[p].n)
            for i in range(NCH):
                if i >= 2:
                    sync.wait_ge(tt.h, TT_CH * (i - 1))
                p = i % 2
                ldr[p].inc(sync.dma_start(out=rhs[p][:],
                                          in_=xlnT_v[:, :, i * CH:(i + 1) * CH]), 16)
                lr_a.append(ldr[p].n)
            for kt in range(KT):
                sync.wait_ge(vv.h, VPRE + VCH * NCH + max(0, kt - 1))
                p = kt % 2
                ldw[p].inc(sync.dma_start(out=wtmp[p][:], in_=Wz_v[:, kt, :]), 16)
                lw_wz.append(ldw[p].n)
            for i in range(NCH):
                sync.wait_ge(tt.h, TTA_TOT + ZTT_CH * max(0, i - 1))
                p = i % 2
                ldr[p].inc(sync.dma_start(out=rhs[p][:],
                                          in_=xlnT_v[:, :, i * CH:(i + 1) * CH]), 16)
                lr_b.append(ldr[p].n)

        @block.vector
        def _(vector):
            vector.wait_ge(ldc.h, 32)
            for md in range(MD):
                vector.memset(xcv[:, md * SLOT:md * SLOT + 3], 0.0)
            vector.tensor_copy(sfl[:, 256:512], sfl[:, 0:256])
            for j in range(4):
                vector.wait_ge(ldw[j % 2].h, lw_xpj[j])
                vv.inc(vector.tensor_copy(xpj[:, j * 576:(j + 1) * 576], wtmp[j % 2][:, 0:576]))
            for kt in range(KT):
                vector.wait_ge(ldw[kt % 2].h, lw_wx[kt])
                vv.inc(vector.tensor_copy(wb[:, kt * DPC:(kt + 1) * DPC], wtmp[kt % 2][:]))
            for i in range(NCH):
                if i >= 1:
                    vector.wait_ge(ss.h, SS_CH * (i - 1) + 9)
                    vector.wait_ge(stx.h, 16 * i)
                for md in range(MD):
                    vector.wait_ge(ss.h, SS_CH * i + 1 + md)
                    base = md * SLOT
                    ob = md * CH
                    vector.tensor_scalar(xcf[:, ob:ob + CH], xcv[:, base:base + CH],
                                         cw[:, md * 4:md * 4 + 1], cb[:, md:md + 1],
                                         ALU.mult, ALU.add)
                    for k in range(1, 4):
                        vector.scalar_tensor_tensor(xcf[:, ob:ob + CH],
                                                    xcv[:, base + k:base + k + CH],
                                                    cw[:, md * 4 + k:md * 4 + k + 1],
                                                    xcf[:, ob:ob + CH],
                                                    ALU.mult, ALU.add)
                vv.inc(vector.tensor_copy(sinkV[:, 0:1], xcf[:, MD * CH - 1:MD * CH]))
                for md in range(MD):
                    base = md * SLOT
                    if i == NCH // 2 - 1 or i == NCH - 1:
                        vector.memset(xcv[:, base:base + 3], 0.0)
                    else:
                        vector.tensor_copy(xcv[:, base:base + 3], xcv[:, base + CH:base + CH + 3])
                vv.inc(vector.tensor_copy(sinkV[:, 1:2], xcv[:, 0:1]))
                vector.wait_ge(ss.h, SS_CH * i + 9)
                if i >= 1:
                    vector.wait_ge(tt.h, TT_CH * i)
                vv.inc(vector.tensor_copy(xcb[:], xcf[:]))
            for kt in range(KT):
                vector.wait_ge(ldw[kt % 2].h, lw_wz[kt])
                if kt == 0:
                    vector.wait_ge(tt.h, TTA_TOT)
                vv.inc(vector.tensor_copy(wb[:, kt * DPC:(kt + 1) * DPC], wtmp[kt % 2][:]))

        @block.scalar
        def _(scalar):
            scalar.memzero(sfl[:])
            scalar.wait_ge(ldc.h, 32)
            for i in range(NCH):
                for md in range(MD):
                    scalar.wait_ge(tt.h, TT_CH * i + md + 1)
                    if i >= 1 and md == 0:
                        scalar.wait_ge(vv.h, VPRE + VCH * (i - 1) + 2)
                    base = md * SLOT
                    ss.inc(scalar.copy(xcv[:, base + 3:base + 3 + CH], ps[md % 2][:]))
                scalar.wait_ge(vv.h, VPRE + VCH * i + 2)
                ss.inc(scalar.activation(xcf[:], xcf[:], AF.Silu))
                for t4 in range(4):
                    scalar.wait_ge(tt.h, TT_CH * i + 9 + t4)
                    if i >= 2:
                        scalar.wait_ge(stp[i % 2].h, 16 * (i // 2))
                    ss.inc(scalar.copy(sstg[i % 2][:, t4 * 288:(t4 + 1) * 288], ps2[t4 % 2][:]))
                scalar.copy(sfl[:, 256:512], sfl[:, 0:256])
                ss.inc(scalar.copy(sinkS[:, 0:1], sstg[i % 2][:, 0:1]))
            for i in range(NCH):
                for md in range(MD):
                    scalar.wait_ge(tt.h, TTA_TOT + ZTT_CH * i + md + 1)
                    if i >= 1 and md == 0:
                        scalar.wait_ge(stz.h, 16 * i)
                    ss.inc(scalar.copy(zstg[:, md * CH:(md + 1) * CH], ps[md % 2][:]))
                scalar.copy(sfl[:, 256:512], sfl[:, 0:256])
                ss.inc(scalar.copy(sinkS[:, 1:2], zstg[:, 0:1]))

        @block.tensor
        def _(tensor):
            tensor.wait_ge(vv.h, VPRE)
            for i in range(NCH):
                tensor.wait_ge(ldr[i % 2].h, lr_a[i])
                for md in range(MD):
                    w = SS_CH * i + md - 1 if md >= 2 else SS_CH * (i - 1) + md + 7
                    if w > 0:
                        tensor.wait_ge(ss.h, w)
                    for kt in range(KT):
                        tt_i = tensor.matmul(ps[md % 2][:],
                                             wb[:, kt * DPC + md * 128:kt * DPC + (md + 1) * 128],
                                             rhs[i % 2][:, kt * CH:(kt + 1) * CH],
                                             start=(kt == 0), stop=(kt == KT - 1))
                    tt.inc(tt_i)
                tensor.wait_ge(vv.h, VPRE + VCH * i + 3)
                for t4 in range(4):
                    if t4 >= 2:
                        w = SS_CH * i + 10 + t4 - 2
                    elif i >= 1:
                        w = SS_CH * (i - 1) + 10 + t4 + 2
                    else:
                        w = 0
                    if w > 0:
                        tensor.wait_ge(ss.h, w)
                    for md in range(MD):
                        tt_i = tensor.matmul(ps2[t4 % 2][:],
                                             xcb[:, md * CH + t4 * 128:md * CH + t4 * 128 + 128],
                                             xpj[:, md * 288:(md + 1) * 288],
                                             start=(md == 0), stop=(md == MD - 1))
                    tt.inc(tt_i)
            tensor.wait_ge(vv.h, VPRE + VCH * NCH + KT)
            for i in range(NCH):
                tensor.wait_ge(ldr[i % 2].h, lr_b[i])
                for md in range(MD):
                    if md >= 2:
                        w = SSA_TOT + ZSS_CH * i + md - 1
                    elif i >= 1:
                        w = SSA_TOT + ZSS_CH * (i - 1) + md + 7
                    else:
                        w = SSA_TOT
                    tensor.wait_ge(ss.h, w)
                    for kt in range(KT):
                        tt_i = tensor.matmul(ps[md % 2][:],
                                             wb[:, kt * DPC + md * 128:kt * DPC + (md + 1) * 128],
                                             rhs[i % 2][:, kt * CH:(kt + 1) * CH],
                                             start=(kt == 0), stop=(kt == KT - 1))
                    tt.inc(tt_i)

        @block.gpsimd
        def _(gpsimd):
            for i in range(NCH):
                gpsimd.wait_ge(ss.h, SS_CH * i + 9)
                stx.inc(gpsimd.dma_start(
                    out=xc_v[:, :, i * CH:(i + 1) * CH],
                    in_=xcf[:].rearrange("p (md t) -> p md t", t=CH)), 16)
                gpsimd.wait_ge(ss.h, SS_CH * i + 14)
                stp[i % 2].inc(gpsimd.dma_start(
                    out=ssmp_v[:, i * 4:(i + 1) * 4, :],
                    in_=sstg[i % 2][:].rearrange("p (t4 e) -> p t4 e", e=288)), 16)
            for i in range(NCH):
                gpsimd.wait_ge(ss.h, SSA_TOT + ZSS_CH * (i + 1))
                stz.inc(gpsimd.dma_start(
                    out=z_v[:, :, i * CH:(i + 1) * CH],
                    in_=zstg[:].rearrange("p (md t) -> p md t", t=CH)), 16)
            if debug:
                stz.inc(gpsimd.dma_start(out=wbd, in_=wb[:]), 16)
            gpsimd.wait_ge(stx.h, stx.n)
            gpsimd.wait_ge(stp[0].h, stp[0].n)
            gpsimd.wait_ge(stp[1].h, stp[1].n)
            gpsimd.wait_ge(stz.h, stz.n)

    return nc



def build_dtpath():
    nc = bass.Bass("TRN2", target_bir_lowering=False, **BASS_KW)
    ssmp = nc.dram_tensor("ssmp", [8, T, 288], F32, kind="ExternalInput").ap()
    lnw = nc.dram_tensor("lnw", [1, 288], F32, kind="ExternalInput").ap()
    dtprojT = nc.dram_tensor("dtprojT", [256, DPC], F32, kind="ExternalInput").ap()
    dtb = nc.dram_tensor("dtb", [128, DPC // 128], F32, kind="ExternalInput").ap()
    onesF = nc.dram_tensor("onesF", [1, 128], F32, kind="ExternalInput").ap()
    idb = nc.dram_tensor("idb", [128, 128], BF16, kind="ExternalInput").ap()
    idf = nc.dram_tensor("idf", [128, 128], F32, kind="ExternalInput").ap()
    cst = nc.dram_tensor("cst", [128, 4], F32, kind="ExternalInput").ap()
    dt_o = nc.dram_tensor("dt", [DPC, T], F32, kind="ExternalOutput").ap()
    bc_o = nc.dram_tensor("BC", [32, T], F32, kind="ExternalOutput").ap()

    ssmp_v = ssmp.rearrange("k (nt p) e -> p nt k e", p=128)
    dt_v = dt_o.rearrange("(md p) t -> p md t", p=128)
    NTL = T // 128
    NCH = T // 512
    MD = DPC // 128

    SPRE, VPRE, TPRE = 1, 1, 1
    def s1(i): return SPRE + 20 * (i // 4) + 3 * (i % 4) + 1
    def s2(i): return s1(i) + 1
    def s3(i): return s1(i) + 2
    def s4(c, md): return SPRE + 20 * c + 12 + md + 1
    def v1(i): return VPRE + 21 * (i // 4) + 3 * (i % 4) + 1
    def v2(i): return v1(i) + 1
    def v3(i): return v1(i) + 2
    def v4(c, md): return VPRE + 21 * c + 12 + md + 1
    def vsink(c): return VPRE + 21 * c + 21
    def t1(i): return TPRE + 12 * (i // 4) + (i % 4) + 1
    def t2(c, md): return TPRE + 12 * c + 4 + md + 1

    with ExitStack() as ctx:
        pbuf = [ctx.enter_context(nc.sbuf_tensor(f"pbuf{i}", [128, 8 * 288], F32)) for i in range(2)]
        acc = [ctx.enter_context(nc.sbuf_tensor(f"acc{i}", [128, 288], F32)) for i in range(2)]
        sq = [ctx.enter_context(nc.sbuf_tensor(f"sq{i}", [128, 288], F32)) for i in range(2)]
        nrm0 = ctx.enter_context(nc.sbuf_tensor("nrm0", [128, 288], F32))
        nrmb = ctx.enter_context(nc.sbuf_tensor("nrmb", [128, 256], BF16))
        nrmf = ctx.enter_context(nc.sbuf_tensor("nrmf", [128, 32], F32))
        slnw = ctx.enter_context(nc.sbuf_tensor("slnw", [1, 288], F32))
        sones = ctx.enter_context(nc.sbuf_tensor("sones", [1, 128], F32))
        sidb = ctx.enter_context(nc.sbuf_tensor("sidb", [128, 128], BF16))
        sidf = ctx.enter_context(nc.sbuf_tensor("sidf", [128, 128], F32))
        scst = ctx.enter_context(nc.sbuf_tensor("scst", [128, 4], F32))
        sdtb = ctx.enter_context(nc.sbuf_tensor("sdtb", [128, MD], F32))
        lnwb = ctx.enter_context(nc.sbuf_tensor("lnwb", [128, 288], F32))
        dtmp = ctx.enter_context(nc.sbuf_tensor("dtmp", [128, 2048], F32))
        dpj = ctx.enter_context(nc.sbuf_tensor("dpj", [128, 2048], BF16))
        dtrT = [ctx.enter_context(nc.sbuf_tensor(f"dtrT{i}", [128, 1024], BF16)) for i in range(2)]
        bcstg = [ctx.enter_context(nc.sbuf_tensor(f"bcstg{i}", [32, 512], F32)) for i in range(2)]
        dtstg = [ctx.enter_context(nc.sbuf_tensor(f"dtstg{i}", [128, MD * 512], F32)) for i in range(2)]
        spA = [ctx.enter_context(nc.sbuf_tensor(f"spA{i}", [128, 512], F32)) for i in range(2)]
        spV = [ctx.enter_context(nc.sbuf_tensor(f"spV{i}", [128, 512], F32)) for i in range(2)]
        statV = ctx.enter_context(nc.sbuf_tensor("statV", [128, 8], F32))
        statS = ctx.enter_context(nc.sbuf_tensor("statS", [128, 4], F32))
        vfl = ctx.enter_context(nc.sbuf_tensor("vfl", [128, 512], F32))
        sfl = ctx.enter_context(nc.sbuf_tensor("sfl", [128, 512], F32))
        sinkV = ctx.enter_context(nc.sbuf_tensor("sinkV", [128, 4], F32))
        sinkS = ctx.enter_context(nc.sbuf_tensor("sinkS", [128, 4], F32))
        psT = [ctx.enter_context(nc.psum_tensor(f"psT{i}", [128, 128], BF16)) for i in range(2)]
        psF = ctx.enter_context(nc.psum_tensor("psF", [32, 128], F32))
        lnwps = ctx.enter_context(nc.psum_tensor("lnwps", [128, 288], F32))
        psD = [ctx.enter_context(nc.psum_tensor(f"psD{i}", [128, 512], F32)) for i in range(2)]
        ldc = Sem(nc, ctx, "ldc")
        ldp = [Sem(nc, ctx, "ldp0"), Sem(nc, ctx, "ldp1")]
        vv = Sem(nc, ctx, "vv")
        ss = Sem(nc, ctx, "ss")
        tt = Sem(nc, ctx, "tt")
        std = [Sem(nc, ctx, "std0"), Sem(nc, ctx, "std1")]
        stb = [Sem(nc, ctx, "stb0"), Sem(nc, ctx, "stb1")]
        block = ctx.enter_context(nc.Block())

        @block.sync
        def _(sync):
            for ap_in, dst in [(lnw, slnw), (onesF, sones), (idb, sidb), (idf, sidf),
                               (cst, scst), (dtb, sdtb)]:
                ldc.inc(sync.dma_start(out=dst[:], in_=ap_in), 16)
            ldc.inc(sync.dma_start(out=dtmp[:, 0:1024], in_=dtprojT[0:128, :]), 16)
            ldc.inc(sync.dma_start(out=dtmp[:, 1024:2048], in_=dtprojT[128:256, :]), 16)
            for i in range(NTL):
                if i >= 2:
                    sync.wait_ge(vv.h, v1(i - 2))
                ldp[i % 2].inc(sync.dma_start(
                    out=pbuf[i % 2][:].rearrange("p (k e) -> p k e", e=288),
                    in_=ssmp_v[:, i, :, :]), 16)

        @block.vector
        def _(vector):
            vector.memset(vfl[:], 0.0)
            vector.wait_ge(ldc.h, 16 * 8)
            vv.inc(vector.tensor_copy(dpj[:], dtmp[:]))
            for c in range(NCH):
                for t4 in range(4):
                    i = 4 * c + t4
                    a = acc[i % 2]
                    vector.wait_ge(ldp[i % 2].h, 16 * (i // 2 + 1))
                    if i >= 2:
                        vector.wait_ge(ss.h, s1(i - 2))
                    pb = pbuf[i % 2]
                    vector.tensor_add(a[:], pb[:, 0:288], pb[:, 288:576])
                    for k in range(2, 8):
                        vector.tensor_add(a[:], a[:], pb[:, k * 288:(k + 1) * 288])
                    vector.tensor_copy(vfl[:, 256:512], vfl[:, 0:256])
                    vv.inc(vector.tensor_copy(sinkV[:, 0:1], a[:, 0:1]))
                    vector.wait_ge(ss.h, s1(i))
                    s = sq[i % 2]
                    vector.tensor_reduce(statV[:, 0:1], s[:, 0:256], AX.X, ALU.add)
                    vector.tensor_reduce(statV[:, 1:2], s[:, 256:272], AX.X, ALU.add)
                    vector.tensor_reduce(statV[:, 2:3], s[:, 272:288], AX.X, ALU.add)
                    vector.tensor_copy(vfl[:, 256:512], vfl[:, 0:256])
                    vector.tensor_scalar(statV[:, 3:4], statV[:, 0:1], 1.0 / 256, None, ALU.mult)
                    vector.tensor_scalar(statV[:, 4:5], statV[:, 1:2], 1.0 / 16, None, ALU.mult)
                    vector.tensor_scalar(statV[:, 5:6], statV[:, 2:3], 1.0 / 16, None, ALU.mult)
                    vector.tensor_copy(vfl[:, 256:512], vfl[:, 0:256])
                    vv.inc(vector.tensor_copy(sinkV[:, 1:2], statV[:, 5:6]))
                    vector.wait_ge(ss.h, s2(i))
                    if i >= 1:
                        vector.wait_ge(tt.h, t1(i - 1))
                    vector.tensor_scalar(nrm0[:, 0:256], a[:, 0:256], statS[:, 0:1], None, ALU.mult)
                    vector.tensor_scalar(nrm0[:, 256:272], a[:, 256:272], statS[:, 1:2], None, ALU.mult)
                    vector.tensor_scalar(nrm0[:, 272:288], a[:, 272:288], statS[:, 2:3], None, ALU.mult)
                    vector.tensor_mul(nrmb[:], nrm0[:, 0:256], lnwb[:, 0:256])
                    vector.tensor_mul(nrmf[:], nrm0[:, 256:288], lnwb[:, 256:288])
                    vector.tensor_copy(vfl[:, 256:512], vfl[:, 0:256])
                    vv.inc(vector.tensor_copy(sinkV[:, 2:3], nrmf[:, 0:1]))
                for md in range(MD):
                    vector.wait_ge(tt.h, t2(c, md))
                    vector.wait_ge(ss.h, s4(c, md))
                    if md == 0 and c >= 2:
                        vector.wait_ge(std[c % 2].h, 16 * (c // 2))
                    vector.tensor_scalar(spV[md % 2][:], psD[md % 2][:],
                                         sdtb[:, md:md + 1], 0.0, ALU.add, ALU.max)
                    vv.inc(vector.tensor_add(dtstg[c % 2][:, md * 512:(md + 1) * 512],
                                             spV[md % 2][:], spA[md % 2][:]))
                vector.tensor_copy(vfl[:, 256:512], vfl[:, 0:256])
                vv.inc(vector.tensor_copy(sinkV[:, 3:4], dtstg[c % 2][:, 0:1]))

        @block.scalar
        def _(scalar):
            scalar.memzero(sfl[:])
            scalar.wait_ge(ldc.h, 16 * 8)
            scalar.wait_ge(tt.h, 1)
            scalar.copy(lnwb[:], lnwps[:])
            scalar.copy(sfl[:, 256:512], sfl[:, 0:256])
            ss.inc(scalar.copy(sinkS[:, 0:1], lnwb[:, 0:1]))
            for c in range(NCH):
                for t4 in range(4):
                    i = 4 * c + t4
                    scalar.wait_ge(vv.h, v1(i))
                    if i >= 2:
                        scalar.wait_ge(vv.h, v2(i - 2))
                    scalar.activation(sq[i % 2][:], acc[i % 2][:], AF.Square)
                    scalar.copy(sfl[:, 256:512], sfl[:, 0:256])
                    ss.inc(scalar.copy(sinkS[:, 1:2], sq[i % 2][:, 0:1]))
                    scalar.wait_ge(vv.h, v2(i))
                    scalar.activation(statS[:, 0:1], statV[:, 3:4], AF.Ln, bias=scst[:, 0:1])
                    scalar.activation(statS[:, 0:1], statS[:, 0:1], AF.Exp, scale=-0.5)
                    scalar.activation(statS[:, 1:2], statV[:, 4:5], AF.Ln, bias=scst[:, 0:1])
                    scalar.activation(statS[:, 1:2], statS[:, 1:2], AF.Exp, scale=-0.5)
                    scalar.activation(statS[:, 2:3], statV[:, 5:6], AF.Ln, bias=scst[:, 0:1])
                    scalar.activation(statS[:, 2:3], statS[:, 2:3], AF.Exp, scale=-0.5)
                    scalar.copy(sfl[:, 256:512], sfl[:, 0:256])
                    ss.inc(scalar.copy(sinkS[:, 2:3], statS[:, 2:3]))
                    scalar.wait_ge(tt.h, t1(i))
                    if c >= 2 and t4 == 0:
                        scalar.wait_ge(tt.h, t2(c - 2, MD - 1))
                    if t4 == 0 and c >= 2:
                        scalar.wait_ge(stb[c % 2].h, 16 * (c // 2))
                    scalar.copy(dtrT[c % 2][:, t4 * 128:(t4 + 1) * 128], psT[0][:])
                    scalar.copy(dtrT[c % 2][:, 512 + t4 * 128:512 + (t4 + 1) * 128], psT[1][:])
                    scalar.copy(bcstg[c % 2][:, t4 * 128:(t4 + 1) * 128], psF[:])
                    scalar.copy(sfl[:, 256:512], sfl[:, 0:256])
                    ss.inc(scalar.copy(sinkS[0:1, 3:4], bcstg[c % 2][0:1, 0:1]))
                for md in range(MD):
                    scalar.wait_ge(tt.h, t2(c, md))
                    if md >= 2:
                        scalar.wait_ge(vv.h, v4(c, md - 2))
                    elif c >= 1:
                        scalar.wait_ge(vv.h, v4(c - 1, md + 6))
                    scalar.activation(spA[md % 2][:], psD[md % 2][:], AF.Abs, bias=sdtb[:, md:md + 1])
                    scalar.activation(spA[md % 2][:], spA[md % 2][:], AF.Exp, scale=-1.0)
                    scalar.activation(spA[md % 2][:], spA[md % 2][:], AF.Ln, bias=scst[:, 1:2])
                    scalar.copy(sfl[:, 256:512], sfl[:, 0:256])
                    ss.inc(scalar.copy(sinkS[:, 1:2], spA[md % 2][:, 0:1]))

        @block.tensor
        def _(tensor):
            tensor.wait_ge(ldc.h, 16 * 8)
            tt.inc(tensor.matmul(lnwps[:], sones[:, 0:128], slnw[:], start=True, stop=True))
            for c in range(NCH):
                for t4 in range(4):
                    i = 4 * c + t4
                    tensor.wait_ge(vv.h, v3(i))
                    if i >= 1:
                        tensor.wait_ge(ss.h, s3(i - 1))
                    tensor.transpose(psT[0][:], nrmb[:, 0:128], sidb[:])
                    tensor.transpose(psT[1][:], nrmb[:, 128:256], sidb[:])
                    tt.inc(tensor.transpose(psF[:], nrmf[:], sidf[:]))
                for md in range(MD):
                    if md == 0:
                        tensor.wait_ge(ss.h, s3(4 * c + 3))
                    if md >= 2:
                        tensor.wait_ge(ss.h, s4(c, md - 2))
                        tensor.wait_ge(vv.h, v4(c, md - 2))
                    elif c >= 1:
                        tensor.wait_ge(ss.h, s4(c - 1, md + 6))
                        tensor.wait_ge(vv.h, v4(c - 1, md + 6))
                    for k in range(2):
                        mm = tensor.matmul(psD[md % 2][:],
                                           dpj[:, k * 1024 + md * 128:k * 1024 + (md + 1) * 128],
                                           dtrT[c % 2][:, k * 512:(k + 1) * 512],
                                           start=(k == 0), stop=(k == 1))
                    tt.inc(mm)

        @block.gpsimd
        def _(gpsimd):
            for c in range(NCH):
                gpsimd.wait_ge(ss.h, s3(4 * c + 3))
                stb[c % 2].inc(gpsimd.dma_start(
                    out=bc_o[:, c * 512:(c + 1) * 512], in_=bcstg[c % 2][:]), 16)
                gpsimd.wait_ge(vv.h, vsink(c))
                std[c % 2].inc(gpsimd.dma_start(
                    out=dt_v[:, :, c * 512:(c + 1) * 512],
                    in_=dtstg[c % 2][:].rearrange("p (md t) -> p md t", t=512)), 16)
            gpsimd.wait_ge(std[0].h, std[0].n)
            gpsimd.wait_ge(std[1].h, std[1].n)
            gpsimd.wait_ge(stb[0].h, stb[0].n)
            gpsimd.wait_ge(stb[1].h, stb[1].n)

    return nc



def build_scan():
    nc = bass.Bass("TRN2", target_bir_lowering=False, **BASS_KW)
    xc_i = nc.dram_tensor("xc", [DPC, T], F32, kind="ExternalInput").ap()
    z_i = nc.dram_tensor("z", [DPC, T], F32, kind="ExternalInput").ap()
    dt_i = nc.dram_tensor("dt", [DPC, T], F32, kind="ExternalInput").ap()
    bc_i = nc.dram_tensor("BC", [32, T], F32, kind="ExternalInput").ap()
    al_i = nc.dram_tensor("ALpm", [128, 128], F32, kind="ExternalInput").ap()
    d_i = nc.dram_tensor("Dpm", [128, 8], F32, kind="ExternalInput").ap()
    onesF = nc.dram_tensor("onesF", [1, 128], F32, kind="ExternalInput").ap()
    y2_o = nc.dram_tensor("y2", [DPC, T], BF16, kind="ExternalOutput").ap()

    MD = DPC // 128
    LC = 256
    NCH = T // LC
    NS = NST
    xc_v = xc_i.rearrange("(md p) t -> p md t", p=128)
    z_v = z_i.rearrange("(md p) t -> p md t", p=128)
    dt_v = dt_i.rearrange("(md p) t -> p md t", p=128)
    y2_v = y2_o.rearrange("(md p) t -> p md t", p=128)

    def vh(c, md): return 17 * c + 2 * md + 1
    def vy(c, md): return 17 * c + 2 * md + 2
    def vsink(c): return 17 * c + 17
    def sevac(c, j): return 42 * c + j + 1
    def sesink(c): return 42 * c + 33
    def szl(c): return 42 * c + 34
    def sdA(c, md): return 42 * c + 35 + md
    def tb(c, j): return 32 * c + j + 1
    VPRE_V = 1
    SPRE = 1
    VPRE_S = SPRE

    with ExitStack() as ctx:
        dtb_ = [ctx.enter_context(nc.sbuf_tensor(f"dtb{i}", [128, MD * LC], F32)) for i in range(2)]
        xcb_ = [ctx.enter_context(nc.sbuf_tensor(f"xcb{i}", [128, MD * LC], F32)) for i in range(2)]
        zb = ctx.enter_context(nc.sbuf_tensor("zb", [128, MD * LC], F32))
        bcf = ctx.enter_context(nc.sbuf_tensor("bcf", [1, 32 * LC], F32))
        bex = ctx.enter_context(nc.sbuf_tensor("bex", [128, NS * LC], F32))
        cex = ctx.enter_context(nc.sbuf_tensor("cex", [128, NS * LC], F32))
        dA = ctx.enter_context(nc.sbuf_tensor("dA", [128, NS * LC], F32))
        dBx = ctx.enter_context(nc.sbuf_tensor("dBx", [128, NS * LC], F32))
        hb = ctx.enter_context(nc.sbuf_tensor("hb", [128, NS * LC], F32))
        dtx = ctx.enter_context(nc.sbuf_tensor("dtx", [128, LC], F32))
        yv = ctx.enter_context(nc.sbuf_tensor("yv", [128, LC], F32))
        hcar = ctx.enter_context(nc.sbuf_tensor("hcar", [128, MD * NS], F32))
        y2s = [ctx.enter_context(nc.sbuf_tensor(f"y2s{i}", [128, MD * LC], BF16)) for i in range(2)]
        altmp = ctx.enter_context(nc.sbuf_tensor("altmp", [128, 128], F32))
        asb = ctx.enter_context(nc.sbuf_tensor("asb", [128, 128], F32))
        dsb = ctx.enter_context(nc.sbuf_tensor("dsb", [128, 8], F32))
        sonesF = ctx.enter_context(nc.sbuf_tensor("sonesF", [1, 128], F32))
        vfl = ctx.enter_context(nc.sbuf_tensor("vfl", [128, 512], F32))
        sfl = ctx.enter_context(nc.sbuf_tensor("sfl", [128, 512], F32))
        sinkV = ctx.enter_context(nc.sbuf_tensor("sinkV", [128, 4], F32))
        sinkS = ctx.enter_context(nc.sbuf_tensor("sinkS", [128, 4], F32))
        psB = [ctx.enter_context(nc.psum_tensor(f"psB{i}", [128, LC], F32)) for i in range(2)]
        ldc = Sem(nc, ctx, "ldc")
        ldd = [Sem(nc, ctx, "ldd0"), Sem(nc, ctx, "ldd1")]
        ldx = [Sem(nc, ctx, "ldx0"), Sem(nc, ctx, "ldx1")]
        ldz = Sem(nc, ctx, "ldz")
        ldb = Sem(nc, ctx, "ldb")
        vv = Sem(nc, ctx, "vv")
        ss = Sem(nc, ctx, "ss")
        tt = Sem(nc, ctx, "tt")
        sty = [Sem(nc, ctx, "sty0"), Sem(nc, ctx, "sty1")]
        block = ctx.enter_context(nc.Block())

        @block.sync
        def _(sync):
            ldc.inc(sync.dma_start(out=altmp[:], in_=al_i), 16)
            ldc.inc(sync.dma_start(out=dsb[:], in_=d_i), 16)
            ldc.inc(sync.dma_start(out=sonesF[:], in_=onesF), 16)
            for c in range(NCH):
                p = c % 2
                if c >= 2:
                    sync.wait_ge(ss.h, SPRE + sdA(c - 2, 7))
                    sync.wait_ge(vv.h, VPRE_V + vh(c - 2, 7))
                ldd[p].inc(sync.dma_start(
                    out=dtb_[p][:].rearrange("p (md t) -> p md t", t=LC),
                    in_=dt_v[:, :, c * LC:(c + 1) * LC]), 16)
                if c >= 2:
                    sync.wait_ge(vv.h, VPRE_V + vy(c - 2, 7))
                ldx[p].inc(sync.dma_start(
                    out=xcb_[p][:].rearrange("p (md t) -> p md t", t=LC),
                    in_=xc_v[:, :, c * LC:(c + 1) * LC]), 16)
                if c >= 1:
                    sync.wait_ge(vv.h, VPRE_V + vy(c - 1, 7))
                ldz.inc(sync.dma_start(
                    out=zb[:].rearrange("p (md t) -> p md t", t=LC),
                    in_=z_v[:, :, c * LC:(c + 1) * LC]), 16)
                if c >= 1:
                    sync.wait_ge(tt.h, tb(c - 1, 31))
                ldb.inc(sync.dma_start(
                    out=bcf[:].rearrange("o (r t) -> o r t", t=LC),
                    in_=bc_i[:, c * LC:(c + 1) * LC]), 16)

        @block.vector
        def _(vector):
            vector.memset(vfl[:], 0.0)
            vector.wait_ge(ldc.h, 48)
            vector.wait_ge(ss.h, SPRE)
            vector.tensor_scalar(asb[:], altmp[:], -1.0, None, ALU.mult)
            vector.tensor_copy(vfl[:, 256:512], vfl[:, 0:256])
            vv.inc(vector.tensor_copy(sinkV[:, 0:1], asb[:, 0:1]))
            for c in range(NCH):
                p = c % 2
                vector.wait_ge(ldd[p].h, 16 * (c // 2 + 1))
                vector.wait_ge(ldx[p].h, 16 * (c // 2 + 1))
                vector.wait_ge(ss.h, VPRE_S + sesink(c))
                for md in range(MD):
                    vector.tensor_mul(dtx[:], dtb_[p][:, md * LC:(md + 1) * LC],
                                      xcb_[p][:, md * LC:(md + 1) * LC])
                    for n in range(NS):
                        vector.tensor_mul(dBx[:, n * LC:(n + 1) * LC], dtx[:],
                                          bex[:, n * LC:(n + 1) * LC])
                    vector.wait_ge(ss.h, VPRE_S + sdA(c, md))
                    for n in range(NS):
                        init = 0.0 if c % 16 == 0 else hcar[:, md * NS + n:md * NS + n + 1]
                        vector.tensor_tensor_scan(hb[:, n * LC:(n + 1) * LC],
                                                  dA[:, n * LC:(n + 1) * LC],
                                                  dBx[:, n * LC:(n + 1) * LC],
                                                  init, ALU.mult, ALU.add)
                    vector.tensor_copy(vfl[:, 256:512], vfl[:, 0:256])
                    vector.tensor_copy(
                        hcar[:, md * NS:(md + 1) * NS].rearrange("p (n o) -> p n o", o=1),
                        hb.ap().rearrange("p (n t) -> p n t", t=LC)[:, :, LC - 1:LC])
                    vv.inc(vector.tensor_mul(dBx[:], hb[:], cex[:]))
                    vector.tensor_add(dBx[:, 0:8 * LC], dBx[:, 0:8 * LC], dBx[:, 8 * LC:16 * LC])
                    vector.tensor_add(dBx[:, 0:4 * LC], dBx[:, 0:4 * LC], dBx[:, 4 * LC:8 * LC])
                    vector.tensor_add(dBx[:, 0:2 * LC], dBx[:, 0:2 * LC], dBx[:, 2 * LC:4 * LC])
                    vector.tensor_add(dBx[:, 0:LC], dBx[:, 0:LC], dBx[:, LC:2 * LC])
                    vector.scalar_tensor_tensor(yv[:], xcb_[p][:, md * LC:(md + 1) * LC],
                                                dsb[:, md:md + 1], dBx[:, 0:LC],
                                                ALU.mult, ALU.add)
                    if md == 0:
                        vector.wait_ge(ss.h, VPRE_S + szl(c))
                        if c >= 2:
                            vector.wait_ge(sty[p].h, 16 * (c // 2))
                    vv.inc(vector.tensor_mul(y2s[p][:, md * LC:(md + 1) * LC], yv[:],
                                             zb[:, md * LC:(md + 1) * LC]))
                vector.tensor_copy(vfl[:, 256:512], vfl[:, 0:256])
                vv.inc(vector.tensor_copy(sinkV[:, 1:2], y2s[p][:, 0:1]))

        @block.scalar
        def _(scalar):
            scalar.memzero(sfl[:])
            scalar.wait_ge(ldc.h, 48)
            scalar.activation(altmp[:], altmp[:], AF.Exp)
            scalar.copy(sfl[:, 256:512], sfl[:, 0:256])
            ss.inc(scalar.copy(sinkS[:, 0:1], altmp[:, 0:1]))
            for c in range(NCH):
                p = c % 2
                for j in range(32):
                    scalar.wait_ge(tt.h, tb(c, j))
                    if j == 0 and c >= 1:
                        scalar.wait_ge(vv.h, VPRE_V + vh(c - 1, 7))
                    dst = bex if j < 16 else cex
                    nn_ = j if j < 16 else j - 16
                    ss.inc(scalar.copy(dst[:, nn_ * LC:(nn_ + 1) * LC], psB[j % 2][:]))
                scalar.copy(sfl[:, 256:512], sfl[:, 0:256])
                ss.inc(scalar.copy(sinkS[:, 1:2], cex[:, 0:1]))
                scalar.wait_ge(ldz.h, 16 * (c + 1))
                if c >= 1:
                    scalar.wait_ge(vv.h, VPRE_V + vy(c - 1, 7))
                ss.inc(scalar.activation(zb[:], zb[:], AF.Silu))
                scalar.wait_ge(ldd[p].h, 16 * (c // 2 + 1))
                for md in range(MD):
                    if md >= 1:
                        scalar.wait_ge(vv.h, VPRE_V + vh(c, md - 1))
                    elif c >= 1:
                        scalar.wait_ge(vv.h, VPRE_V + vh(c - 1, 7))
                    else:
                        scalar.wait_ge(vv.h, VPRE_V)
                    for n in range(NS):
                        act = scalar.activation(dA[:, n * LC:(n + 1) * LC],
                                                dtb_[p][:, md * LC:(md + 1) * LC],
                                                AF.Exp, scale=asb[:, md * NS + n:md * NS + n + 1])
                    scalar.copy(sfl[:, 256:512], sfl[:, 0:256])
                    ss.inc(scalar.copy(sinkS[:, 2:3], dA[:, 0:1]))

        @block.tensor
        def _(tensor):
            tensor.wait_ge(ldc.h, 48)
            for c in range(NCH):
                tensor.wait_ge(ldb.h, 16 * (c + 1))
                for j in range(32):
                    if j >= 2:
                        tensor.wait_ge(ss.h, SPRE + sevac(c, j - 2))
                    elif c >= 1:
                        tensor.wait_ge(ss.h, SPRE + sevac(c - 1, 30 + j))
                    tt.inc(tensor.matmul(psB[j % 2][:], sonesF[:, 0:128],
                                         bcf[0:1, j * LC:(j + 1) * LC],
                                         start=True, stop=True))

        @block.gpsimd
        def _(gpsimd):
            for c in range(NCH):
                p = c % 2
                gpsimd.wait_ge(vv.h, VPRE_V + vsink(c))
                sty[p].inc(gpsimd.dma_start(
                    out=y2_v[:, :, c * LC:(c + 1) * LC],
                    in_=y2s[p][:].rearrange("p (md t) -> p md t", t=LC)), 16)
            gpsimd.wait_ge(sty[0].h, sty[0].n)
            gpsimd.wait_ge(sty[1].h, sty[1].n)

    return nc



def build_matmul_res_ln(kdim):
    nc = bass.Bass("TRN2", target_bir_lowering=False, **BASS_KW)
    act = nc.dram_tensor("act", [kdim, TPC], BF16, kind="ExternalInput").ap()
    xTi = nc.dram_tensor("xT", [H, TPC], F32, kind="ExternalInput").ap()
    wT = nc.dram_tensor("wT", [kdim, H], F32, kind="ExternalInput").ap()
    onesF = nc.dram_tensor("onesF", [1, 128], F32, kind="ExternalInput").ap()
    onesC = nc.dram_tensor("onesC", [128, 1], F32, kind="ExternalInput").ap()
    cst = nc.dram_tensor("cst", [128, 4], F32, kind="ExternalInput").ap()
    xo = nc.dram_tensor("x1T", [H, TPC], F32, kind="ExternalOutput").ap()
    lno = nc.dram_tensor("x1ln", [H, TPC], BF16, kind="ExternalOutput").ap()

    KD = kdim // 128
    HG = H // 512
    TC = TPC // 512
    act_v = act.rearrange("(kd p) t -> p kd t", p=128)
    xT_v = xTi.rearrange("(ht p) t -> p ht t", p=128)
    xo_v = xo.rearrange("(ht p) t -> p ht t", p=128)
    lno_v = lno.rearrange("(ht p) t -> p ht t", p=128)
    wT_v = wT.rearrange("(kd p) h -> p kd h", p=128)

    TB = KD + 4
    VB = KD + 4
    TBLK = TB * HG + 2
    VBLK = VB * HG + 2
    SBLK = 4 * HG + 2
    def t_kd(tc, hg, kd): return tc * TBLK + hg * TB + kd + 1
    def t_stat(tc, hg, j): return tc * TBLK + hg * TB + KD + j + 1
    def t_bc(tc): return tc * TBLK + TB * HG + 1
    def v_cast(tc, hg, kd): return tc * VBLK + hg * VB + kd + 1
    def v_evac(tc, hg, j): return tc * VBLK + hg * VB + KD + j + 1
    def v_rows(tc): return tc * VBLK + VB * HG + 1
    def v_readps(tc): return tc * VBLK + VB * HG + 2
    def s_sq(tc, hg, j): return tc * SBLK + hg * 4 + j + 1
    def s_rstd(tc): return tc * SBLK + 4 * HG + 1
    def s_mbrb(tc): return tc * SBLK + 4 * HG + 2
    VA_TOT = VBLK * TC
    def v_ln(tc, hg, j): return VA_TOT + (tc * HG + hg) * 4 + j + 1

    with ExitStack() as ctx:
        wtmp = [ctx.enter_context(nc.sbuf_tensor(f"wtmp{i}", [128, 512], F32)) for i in range(2)]
        wtb = [ctx.enter_context(nc.sbuf_tensor(f"wtb{i}", [128, 512], BF16)) for i in range(2)]
        ytile = [ctx.enter_context(nc.sbuf_tensor(f"ytile{i}", [128, 512], BF16)) for i in range(2)]
        xtile = [ctx.enter_context(nc.sbuf_tensor(f"xtile{i}", [128, 512], F32)) for i in range(2)]
        x1stg = [ctx.enter_context(nc.sbuf_tensor(f"x1stg{i}", [128, 512], F32)) for i in range(2)]
        sqstg = [ctx.enter_context(nc.sbuf_tensor(f"sqstg{i}", [128, 512], F32)) for i in range(2)]
        lnstg = [ctx.enter_context(nc.sbuf_tensor(f"lnstg{i}", [128, 512], BF16)) for i in range(2)]
        cstg = [ctx.enter_context(nc.sbuf_tensor(f"cstg{i}", [128, 512], F32)) for i in range(2)]
        xntile = [ctx.enter_context(nc.sbuf_tensor(f"xntile{i}", [128, 512], F32)) for i in range(2)]
        rows = ctx.enter_context(nc.sbuf_tensor("rows", [1, 512 * 8], F32))
        mb = [ctx.enter_context(nc.sbuf_tensor(f"mb{i}", [128, 512], F32)) for i in range(2)]
        rb = [ctx.enter_context(nc.sbuf_tensor(f"rb{i}", [128, 512], F32)) for i in range(2)]
        sonesF = ctx.enter_context(nc.sbuf_tensor("sonesF", [1, 128], F32))
        sonesC = ctx.enter_context(nc.sbuf_tensor("sonesC", [128, 1], F32))
        scst = ctx.enter_context(nc.sbuf_tensor("scst", [128, 4], F32))
        vfl = ctx.enter_context(nc.sbuf_tensor("vfl", [128, 512], F32))
        sfl = ctx.enter_context(nc.sbuf_tensor("sfl", [128, 512], F32))
        sinkV = ctx.enter_context(nc.sbuf_tensor("sinkV", [128, 4], F32))
        sinkS = ctx.enter_context(nc.sbuf_tensor("sinkS", [128, 4], F32))
        psx = [ctx.enter_context(nc.psum_tensor(f"psx{i}", [128, 512], F32)) for i in range(4)]
        pss = ctx.enter_context(nc.psum_tensor("pss", [1, 512], F32))
        psq = ctx.enter_context(nc.psum_tensor("psq", [1, 512], F32))
        psb = [ctx.enter_context(nc.psum_tensor(f"psb{i}", [128, 512], F32)) for i in range(2)]
        ldc = Sem(nc, ctx, "ldc")
        ldw = [Sem(nc, ctx, "ldw0"), Sem(nc, ctx, "ldw1")]
        ldy = [Sem(nc, ctx, "ldy0"), Sem(nc, ctx, "ldy1")]
        ldr = [Sem(nc, ctx, "ldr0"), Sem(nc, ctx, "ldr1")]
        ldn = [Sem(nc, ctx, "ldn0"), Sem(nc, ctx, "ldn1")]
        vv = Sem(nc, ctx, "vv")
        ss = Sem(nc, ctx, "ss")
        tt = Sem(nc, ctx, "tt")
        stx = [Sem(nc, ctx, "stx0"), Sem(nc, ctx, "stx1")]
        stl = [Sem(nc, ctx, "stl0"), Sem(nc, ctx, "stl1")]
        block = ctx.enter_context(nc.Block())

        lw, lyy, lr, lnn = {}, {}, {}, {}

        @block.sync
        def _(sync):
            ldc.inc(sync.dma_start(out=sonesF[:], in_=onesF), 16)
            ldc.inc(sync.dma_start(out=sonesC[:], in_=onesC), 16)
            ldc.inc(sync.dma_start(out=scst[:], in_=cst), 16)
            for tc in range(TC):
                for hg in range(HG):
                    for kd in range(KD):
                        p = kd % 2
                        gk = (tc * HG + hg) * KD + kd
                        if gk >= 2:
                            pk = gk - 2
                            sync.wait_ge(tt.h, t_kd(pk // (HG * KD), (pk // KD) % HG, pk % KD))
                        ldw[p].inc(sync.dma_start(
                            out=wtmp[p][:], in_=wT_v[:, kd, hg * 512:(hg + 1) * 512]), 16)
                        lw[(tc, hg, kd)] = ldw[p].n
                        ldy[p].inc(sync.dma_start(
                            out=ytile[p][:], in_=act_v[:, kd, tc * 512:(tc + 1) * 512]), 16)
                        lyy[(tc, hg, kd)] = ldy[p].n
                    for j in range(4):
                        p = j % 2
                        gi = (tc * HG + hg) * 4 + j
                        if gi >= 2:
                            pj = gi - 2
                            sync.wait_ge(vv.h, v_evac(pj // (HG * 4), (pj // 4) % HG, pj % 4))
                        ldr[p].inc(sync.dma_start(
                            out=xtile[p][:],
                            in_=xT_v[:, hg * 4 + j, tc * 512:(tc + 1) * 512]), 16)
                        lr[(tc, hg, j)] = ldr[p].n
            for tc in range(TC):
                for hg in range(HG):
                    for j in range(4):
                        p = j % 2
                        gi = (tc * HG + hg) * 4 + j
                        if gi >= 2:
                            pj = gi - 2
                            sync.wait_ge(vv.h, v_ln(pj // (HG * 4), (pj // 4) % HG, pj % 4))
                        sync.wait_ge(stx[j % 2].h, 16 * (gi // 2 + 1))
                        ldn[p].inc(sync.dma_start(
                            out=xntile[p][:],
                            in_=xo_v[:, hg * 4 + j, tc * 512:(tc + 1) * 512]), 16)
                        lnn[(tc, hg, j)] = ldn[p].n

        @block.vector
        def _(vector):
            vector.memset(vfl[:], 0.0)
            for tc in range(TC):
                for hg in range(HG):
                    for kd in range(KD):
                        p = kd % 2
                        vector.wait_ge(ldw[p].h, lw[(tc, hg, kd)])
                        gk = (tc * HG + hg) * KD + kd
                        if gk >= 2:
                            pk = gk - 2
                            vector.wait_ge(tt.h, t_kd(pk // (HG * KD), (pk // KD) % HG, pk % KD))
                        vv.inc(vector.tensor_copy(wtb[p][:], wtmp[p][:]))
                    for j in range(4):
                        p = j % 2
                        vector.wait_ge(tt.h, t_kd(tc, hg, KD - 1))
                        vector.wait_ge(ldr[p].h, lr[(tc, hg, j)])
                        gi = (tc * HG + hg) * 4 + j
                        if gi >= 2:
                            pj = gi - 2
                            ptc, prem = pj // (HG * 4), pj % (HG * 4)
                            phg, pjj = prem // 4, prem % 4
                            vector.wait_ge(ss.h, s_sq(ptc, phg, pjj))
                            vector.wait_ge(tt.h, t_stat(ptc, phg, pjj))
                            vector.wait_ge(stx[p].h, 16 * (gi // 2))
                        vv.inc(vector.tensor_add(x1stg[p][:], psx[j][:], xtile[p][:]))
                if True:
                    vector.wait_ge(tt.h, t_stat(tc, HG - 1, 3))
                    o = 2048 * tc
                    vector.tensor_scalar(rows[:, o:o + 512], pss[:], 1.0 / H, None, ALU.mult)
                    vector.tensor_scalar(rows[:, o + 512:o + 1024], psq[:], 1.0 / H, None, ALU.mult)
                    vector.tensor_copy(vfl[:, 256:512], vfl[:, 0:256])
                    vector.tensor_mul(rows[:, o + 1024:o + 1536], rows[:, o:o + 512], rows[:, o:o + 512])
                    vector.tensor_sub(rows[:, o + 1536:o + 2048], rows[:, o + 512:o + 1024], rows[:, o + 1024:o + 1536])
                    vector.tensor_copy(vfl[:, 256:512], vfl[:, 0:256])
                    vv.inc(vector.tensor_copy(sinkV[0:1, 0:1], rows[0:1, o + 1536:o + 1537]))
                    vv.inc(vector.tensor_copy(vfl[:, 0:1], vfl[:, 1:2]))
            for tc in range(TC):
                for hg in range(HG):
                    for j in range(4):
                        p = j % 2
                        vector.wait_ge(ldn[p].h, lnn[(tc, hg, j)])
                        vector.wait_ge(ss.h, s_mbrb(tc))
                        gi = (tc * HG + hg) * 4 + j
                        if gi >= 2:
                            vector.wait_ge(stl[p].h, 16 * (gi // 2))
                        vector.tensor_sub(cstg[p][:], xntile[p][:], mb[tc % 2][:])
                        vv.inc(vector.tensor_mul(lnstg[p][:], cstg[p][:], rb[tc % 2][:]))

        @block.scalar
        def _(scalar):
            scalar.memzero(sfl[:])
            scalar.wait_ge(ldc.h, 48)
            for tc in range(TC):
                for hg in range(HG):
                    for j in range(4):
                        scalar.wait_ge(vv.h, v_evac(tc, hg, j))
                        gi = (tc * HG + hg) * 4 + j
                        if gi >= 2:
                            pj = gi - 2
                            ptc, prem = pj // (HG * 4), pj % (HG * 4)
                            scalar.wait_ge(tt.h, t_stat(ptc, prem // 4, prem % 4))
                        ss.inc(scalar.activation(sqstg[j % 2][:], x1stg[j % 2][:], AF.Square))
                scalar.wait_ge(vv.h, v_rows(tc))
                o = 2048 * tc
                scalar.activation(rows[0:1, o + 1024:o + 1536], rows[0:1, o + 1536:o + 2048],
                                  AF.Ln, bias=scst[0:1, 0:1])
                scalar.copy(sfl[:, 256:512], sfl[:, 0:256])
                scalar.activation(rows[0:1, o + 1024:o + 1536], rows[0:1, o + 1024:o + 1536],
                                  AF.Exp, scale=-0.5)
                scalar.copy(sfl[:, 256:512], sfl[:, 0:256])
                ss.inc(scalar.copy(sinkS[0:1, 0:1], rows[0:1, o + 1024:o + 1025]))
                scalar.wait_ge(tt.h, t_bc(tc) + 1)
                scalar.copy(mb[tc % 2][:], psb[0][:])
                scalar.copy(rb[tc % 2][:], psb[1][:])
                scalar.copy(sfl[:, 256:512], sfl[:, 0:256])
                ss.inc(scalar.copy(sinkS[:, 1:2], rb[tc % 2][:, 0:1]))

        @block.tensor
        def _(tensor):
            tensor.wait_ge(ldc.h, 48)
            for tc in range(TC):
                for hg in range(HG):
                    for kd in range(KD):
                        p = kd % 2
                        tensor.wait_ge(vv.h, v_cast(tc, hg, kd))
                        tensor.wait_ge(ldy[p].h, lyy[(tc, hg, kd)])
                        if kd == 0:
                            for j in range(4):
                                gi = (tc * HG + hg) * 4 + j
                                if gi >= 4:
                                    pj = gi - 4
                                    ptc, prem = pj // (HG * 4), pj % (HG * 4)
                                    tensor.wait_ge(vv.h, v_evac(ptc, prem // 4, prem % 4))
                        for j in range(4):
                            mm = tensor.matmul(psx[j][:],
                                               wtb[p][:, j * 128:(j + 1) * 128],
                                               ytile[p][:],
                                               start=(kd == 0), stop=(kd == KD - 1))
                        tt.inc(mm)
                    for j in range(4):
                        jj = (tc * HG + hg) * 4 + j
                        tensor.wait_ge(vv.h, v_evac(tc, hg, j))
                        tensor.wait_ge(ss.h, s_sq(tc, hg, j))
                        if tc >= 1 and hg == 0 and j == 0:
                            tensor.wait_ge(vv.h, v_readps(tc - 1))
                        tensor.matmul(pss[:], sonesC[:, 0:1], x1stg[j % 2][:],
                                      start=(hg == 0 and j == 0), stop=(hg == HG - 1 and j == 3))
                        tt.inc(tensor.matmul(psq[:], sonesC[:, 0:1], sqstg[j % 2][:],
                                             start=(hg == 0 and j == 0), stop=(hg == HG - 1 and j == 3)))
                tensor.wait_ge(ss.h, s_rstd(tc))
                if tc >= 1:
                    tensor.wait_ge(ss.h, s_mbrb(tc - 1))
                o = 2048 * tc
                tt.inc(tensor.matmul(psb[0][:], sonesF[:, 0:128], rows[0:1, o:o + 512],
                                     start=True, stop=True))
                tt.inc(tensor.matmul(psb[1][:], sonesF[:, 0:128], rows[0:1, o + 1024:o + 1536],
                                     start=True, stop=True))

        @block.gpsimd
        def _(gpsimd):
            for tc in range(TC):
                for hg in range(HG):
                    for j in range(4):
                        p = j % 2
                        gpsimd.wait_ge(vv.h, v_evac(tc, hg, j))
                        stx[p].inc(gpsimd.dma_start(
                            out=xo_v[:, hg * 4 + j, tc * 512:(tc + 1) * 512],
                            in_=x1stg[p][:]), 16)
            for tc in range(TC):
                for hg in range(HG):
                    for j in range(4):
                        p = j % 2
                        gpsimd.wait_ge(vv.h, v_ln(tc, hg, j))
                        stl[p].inc(gpsimd.dma_start(
                            out=lno_v[:, hg * 4 + j, tc * 512:(tc + 1) * 512],
                            in_=lnstg[p][:]), 16)
            for s_ in (stx[0], stx[1], stl[0], stl[1]):
                gpsimd.wait_ge(s_.h, s_.n)

    return nc



def build_ffn0_qkv():
    nc = bass.Bass("TRN2", target_bir_lowering=False, **BASS_KW)
    x1Ti = nc.dram_tensor("x1T", [H, TPC], F32, kind="ExternalInput").ap()
    x1lni = nc.dram_tensor("x1ln", [H, TPC], BF16, kind="ExternalInput").ap()
    qwi = nc.dram_tensor("qw", [H, H], I32, kind="ExternalInput").ap()
    sci = nc.dram_tensor("scales", [H, 32], F32, kind="ExternalInput").ap()
    zri = nc.dram_tensor("zeros", [H, 32], F32, kind="ExternalInput").ap()
    b0i = nc.dram_tensor("bias0", [128, 32], F32, kind="ExternalInput").ap()
    wqi = nc.dram_tensor("wqT", [H, H], F32, kind="ExternalInput").ap()
    wki = nc.dram_tensor("wkT", [H, 1024], F32, kind="ExternalInput").ap()
    wvi = nc.dram_tensor("wvT", [H, 1024], F32, kind="ExternalInput").ap()
    cosi = nc.dram_tensor("cosT", [128, TPC], F32, kind="ExternalInput").ap()
    sini = nc.dram_tensor("sinT", [128, TPC], F32, kind="ExternalInput").ap()
    idb = nc.dram_tensor("idb", [128, 128], BF16, kind="ExternalInput").ap()
    onesF = nc.dram_tensor("onesF", [1, 128], F32, kind="ExternalInput").ap()
    onesC = nc.dram_tensor("onesC", [128, 1], F32, kind="ExternalInput").ap()
    cst = nc.dram_tensor("cst", [128, 4], F32, kind="ExternalInput").ap()
    x2o = nc.dram_tensor("x2T", [H, TPC], F32, kind="ExternalOutput").ap()
    qo = nc.dram_tensor("qT", [H, TPC], BF16, kind="ExternalOutput").ap()
    ko = nc.dram_tensor("kT", [1024, TPC], BF16, kind="ExternalOutput").ap()
    vo = nc.dram_tensor("vT", [1024, TPC], BF16, kind="ExternalOutput").ap()

    KO = 32
    KI = 32
    TC = TPC // 512
    x1T_v = x1Ti.rearrange("(ht p) t -> p ht t", p=128)
    x1ln_v = x1lni.rearrange("(ht p) t -> p ht t", p=128)
    qw_v = qwi.rearrange("(ko p) i -> p ko i", p=128)
    sc_v = sci.rearrange("(ko p) g -> p ko g", p=128)
    zr_v = zri.rearrange("(ko p) g -> p ko g", p=128)
    x2_v = x2o.rearrange("(ht p) t -> p ht t", p=128)
    wq_v = wqi.rearrange("(kd p) m -> p kd m", p=128)
    wk_v = wki.rearrange("(kd p) m -> p kd m", p=128)
    wv_v = wvi.rearrange("(kd p) m -> p kd m", p=128)
    q_vv = qo.rearrange("(mt p) t -> p mt t", p=128)
    k_vv = ko.rearrange("(mt p) t -> p mt t", p=128)
    v_vv = vo.rearrange("(mt p) t -> p mt t", p=128)

    qkv_jobs = [(wq_v, q_vv, 32, True), (wk_v, k_vv, 8, True), (wv_v, v_vv, 8, False)]

    V, S, Tt = {}, {}, {}
    vn = sn = tn = 0
    for ko_ in range(KO):
        vn += 1; V[("cast", ko_)] = vn
        vn += 1; V[("deq", ko_)] = vn
        for ki in range(KI):
            tn += 1; Tt[("tr", ko_, ki)] = tn
            sn += 1; S[("tre", ko_, ki)] = sn
        for tc in range(TC):
            tn += 1; Tt[("ff", ko_, tc)] = tn
            vn += 1; V[("fe", ko_, tc)] = vn
            sn += 1; S[("sq", ko_, tc)] = sn
            tn += 1; Tt[("st", ko_, tc)] = tn
    for tc in range(TC):
        vn += 2; V[("rows", tc)] = vn - 1; V[("rdps", tc)] = vn
        sn += 1; S[("rstd", tc)] = sn
        tn += 1; Tt[("bcm", tc)] = tn
        sn += 1; S[("mb", tc)] = sn
        tn += 1; Tt[("bcr", tc)] = tn
        sn += 1; S[("rb", tc)] = sn
    for ht in range(32):
        for tc in range(TC):
            vn += 1; V[("ln", ht, tc)] = vn
    qkv_seq = []
    for jb, (wv_, ov_, MT, rope) in enumerate(qkv_jobs):
        for mt in range(MT):
            for tc in range(TC):
                qkv_seq.append((jb, mt, tc, rope))
    for u, (jb, mt, tc, rope) in enumerate(qkv_seq):
        for kd in range(KI):
            vn += 1; V[("qc", u, kd)] = vn
            tn += 1; Tt[("qm", u, kd)] = tn
        if rope:
            vn += 1; V[("ro", u)] = vn
        else:
            sn += 1; S[("vc", u)] = sn
    VTOT, STOT, TTOT = vn, sn, tn

    with ExitStack() as ctx:
        qtmp = ctx.enter_context(nc.sbuf_tensor("qtmp", [128, H], I32))
        wqf = ctx.enter_context(nc.sbuf_tensor("wqf", [128, H], F32))
        wqb = ctx.enter_context(nc.sbuf_tensor("wqb", [128, H], BF16))
        scz = ctx.enter_context(nc.sbuf_tensor("scz", [128, 64], F32))
        wko = [ctx.enter_context(nc.sbuf_tensor(f"wko{i}", [128, 32 * 128], BF16)) for i in range(2)]
        lnbuf = ctx.enter_context(nc.sbuf_tensor("lnbuf", [128, 32 * TPC], BF16))
        sb0 = ctx.enter_context(nc.sbuf_tensor("sb0", [128, 32], F32))
        scs = ctx.enter_context(nc.sbuf_tensor("scs", [128, TPC], F32))
        ssn = ctx.enter_context(nc.sbuf_tensor("ssn", [128, TPC], F32))
        sidb = ctx.enter_context(nc.sbuf_tensor("sidb", [128, 128], BF16))
        sonesF = ctx.enter_context(nc.sbuf_tensor("sonesF", [1, 128], F32))
        sonesC = ctx.enter_context(nc.sbuf_tensor("sonesC", [128, 1], F32))
        scst = ctx.enter_context(nc.sbuf_tensor("scst", [128, 4], F32))
        xtile = [ctx.enter_context(nc.sbuf_tensor(f"xtile{i}", [128, 512], F32)) for i in range(2)]
        x2stg = [ctx.enter_context(nc.sbuf_tensor(f"x2stg{i}", [128, 512], F32)) for i in range(2)]
        sqstg = [ctx.enter_context(nc.sbuf_tensor(f"sqstg{i}", [128, 512], F32)) for i in range(2)]
        xntile = [ctx.enter_context(nc.sbuf_tensor(f"xntile{i}", [128, 512], F32)) for i in range(2)]
        cstg = [ctx.enter_context(nc.sbuf_tensor(f"cstg{i}", [128, 512], F32)) for i in range(2)]
        wtmp = [ctx.enter_context(nc.sbuf_tensor(f"wtmp{i}", [128, 128], F32)) for i in range(2)]
        wtb = [ctx.enter_context(nc.sbuf_tensor(f"wtb{i}", [128, 128], BF16)) for i in range(2)]
        qstg = [ctx.enter_context(nc.sbuf_tensor(f"qstg{i}", [128, 512], BF16)) for i in range(2)]
        rt1 = ctx.enter_context(nc.sbuf_tensor("rt1", [128, 512], F32))
        rt2 = ctx.enter_context(nc.sbuf_tensor("rt2", [128, 512], F32))
        rows = ctx.enter_context(nc.sbuf_tensor("rows", [1, 512 * 8], F32))
        mb = [ctx.enter_context(nc.sbuf_tensor(f"mb{i}", [128, 512], F32)) for i in range(2)]
        rb = [ctx.enter_context(nc.sbuf_tensor(f"rb{i}", [128, 512], F32)) for i in range(2)]
        vfl = ctx.enter_context(nc.sbuf_tensor("vfl", [128, 512], F32))
        sfl = ctx.enter_context(nc.sbuf_tensor("sfl", [128, 512], F32))
        sinkV = ctx.enter_context(nc.sbuf_tensor("sinkV", [128, 4], F32))
        sinkS = ctx.enter_context(nc.sbuf_tensor("sinkS", [128, 4], F32))
        psff = [ctx.enter_context(nc.psum_tensor(f"psff{i}", [128, 512], F32)) for i in range(2)]
        pss = [ctx.enter_context(nc.psum_tensor(f"pss{i}", [1, 512], F32)) for i in range(2)]
        psq = [ctx.enter_context(nc.psum_tensor(f"psq{i}", [1, 512], F32)) for i in range(2)]
        psT = ctx.enter_context(nc.psum_tensor("psT", [128, 128], BF16))
        psb = ctx.enter_context(nc.psum_tensor("psb", [128, 512], F32))
        ldc = Sem(nc, ctx, "ldc")
        ldq = Sem(nc, ctx, "ldq")
        ldz = Sem(nc, ctx, "ldz")
        ldr = [Sem(nc, ctx, "ldr0"), Sem(nc, ctx, "ldr1")]
        ldn = [Sem(nc, ctx, "ldn0"), Sem(nc, ctx, "ldn1")]
        ldw = [Sem(nc, ctx, "ldw0"), Sem(nc, ctx, "ldw1")]
        vv = Sem(nc, ctx, "vv")
        ss = Sem(nc, ctx, "ss")
        tt = Sem(nc, ctx, "tt")
        stx = [Sem(nc, ctx, "stx0"), Sem(nc, ctx, "stx1")]
        stq = [Sem(nc, ctx, "stq0"), Sem(nc, ctx, "stq1")]
        block = ctx.enter_context(nc.Block())

        lr, lnn, lw = {}, {}, {}

        @block.sync
        def _(sync):
            for src, dst in [(b0i, sb0), (cosi, scs), (sini, ssn), (idb, sidb),
                             (onesF, sonesF), (onesC, sonesC), (cst, scst)]:
                ldc.inc(sync.dma_start(out=dst[:], in_=src), 16)
            ldc.inc(sync.dma_start(
                out=lnbuf[:].rearrange("p (ht t) -> p ht t", t=TPC), in_=x1ln_v), 16)
            for ko_ in range(KO):
                if ko_ >= 1:
                    sync.wait_ge(vv.h, V[("cast", ko_ - 1)])
                ldq.inc(sync.dma_start(out=qtmp[:], in_=qw_v[:, ko_, :]), 16)
                if ko_ >= 1:
                    sync.wait_ge(vv.h, V[("deq", ko_ - 1)])
                ldz.inc(sync.dma_start(out=scz[:, 0:32], in_=sc_v[:, ko_, :]), 16)
                ldz.inc(sync.dma_start(out=scz[:, 32:64], in_=zr_v[:, ko_, :]), 16)
                for tc in range(TC):
                    gi = ko_ * TC + tc
                    p = gi % 2
                    if gi >= 2:
                        pko, ptc = divmod(gi - 2, TC)
                        sync.wait_ge(vv.h, V[("fe", pko, ptc)])
                    ldr[p].inc(sync.dma_start(
                        out=xtile[p][:], in_=x1T_v[:, ko_, tc * 512:(tc + 1) * 512]), 16)
                    lr[(ko_, tc)] = ldr[p].n
            for ht in range(32):
                for tc in range(TC):
                    gi = ht * TC + tc
                    p = gi % 2
                    if gi >= 2:
                        pht, ptc = divmod(gi - 2, TC)
                        sync.wait_ge(vv.h, V[("ln", pht, ptc)])
                    sync.wait_ge(stx[gi % 2].h, 16 * (gi // 2 + 1))
                    ldn[p].inc(sync.dma_start(
                        out=xntile[p][:], in_=x2_v[:, ht, tc * 512:(tc + 1) * 512]), 16)
                    lnn[(ht, tc)] = ldn[p].n
            for u, (jb, mt, tc, rope) in enumerate(qkv_seq):
                wv_ = qkv_jobs[jb][0]
                for kd in range(KI):
                    gi = u * KI + kd
                    p = gi % 2
                    if gi >= 2:
                        pu, pkd = divmod(gi - 2, KI)
                        sync.wait_ge(vv.h, V[("qc", pu, pkd)])
                    ldw[p].inc(sync.dma_start(
                        out=wtmp[p][:],
                        in_=wv_[:, kd, mt * 128:(mt + 1) * 128]), 16)
                    lw[(u, kd)] = ldw[p].n

        @block.vector
        def _(vector):
            vector.memset(vfl[:], 0.0)
            vector.wait_ge(ldc.h, 16 * 8)
            for ko_ in range(KO):
                vector.wait_ge(ldq.h, 16 * (ko_ + 1))
                if ko_ >= 1:
                    vector.wait_ge(tt.h, Tt[("tr", ko_ - 1, KI - 1)])
                vv.inc(vector.tensor_copy(wqf[:], qtmp[:]))
                vector.wait_ge(ldz.h, 32 * (ko_ + 1))
                for g in range(32):
                    vector.tensor_scalar(wqf[:, g * 128:(g + 1) * 128],
                                         wqf[:, g * 128:(g + 1) * 128],
                                         scz[:, g:g + 1], scz[:, 32 + g:33 + g],
                                         ALU.mult, ALU.add)
                vector.tensor_copy(vfl[:, 256:512], vfl[:, 0:256])
                vv.inc(vector.tensor_copy(wqb[:], wqf[:]))
                for tc in range(TC):
                    p = (ko_ * TC + tc) % 2
                    vector.wait_ge(tt.h, Tt[("ff", ko_, tc)])
                    vector.wait_ge(ldr[p].h, lr[(ko_, tc)])
                    gi = ko_ * TC + tc
                    if gi >= 2:
                        pko, ptc = divmod(gi - 2, TC)
                        vector.wait_ge(ss.h, S[("sq", pko, ptc)])
                        vector.wait_ge(tt.h, Tt[("st", pko, ptc)])
                        vector.wait_ge(stx[p].h, 16 * (gi // 2))
                    vector.scalar_tensor_tensor(x2stg[p][:], psff[tc % 2][:],
                                                sb0[:, ko_:ko_ + 1], xtile[p][:],
                                                ALU.add, ALU.add)
                    vv.inc(vector.tensor_copy(sinkV[:, 0:1], x2stg[p][:, 0:1]))
            for tc in range(TC):
                vector.wait_ge(tt.h, Tt[("st", KO - 1, tc)])
                o = 2048 * tc
                vector.tensor_scalar(rows[:, o:o + 512], pss[tc][:], 1.0 / H, None, ALU.mult)
                vector.tensor_scalar(rows[:, o + 512:o + 1024], psq[tc][:], 1.0 / H, None, ALU.mult)
                vector.tensor_copy(vfl[:, 256:512], vfl[:, 0:256])
                vector.tensor_mul(rows[:, o + 1024:o + 1536], rows[:, o:o + 512], rows[:, o:o + 512])
                vector.tensor_sub(rows[:, o + 1536:o + 2048], rows[:, o + 512:o + 1024], rows[:, o + 1024:o + 1536])
                vector.tensor_copy(vfl[:, 256:512], vfl[:, 0:256])
                vv.inc(vector.tensor_copy(sinkV[0:1, 0:1], rows[0:1, o + 1536:o + 1537]))
                vv.inc(vector.tensor_copy(vfl[:, 0:1], vfl[:, 1:2]))
            for ht in range(32):
                for tc in range(TC):
                    p = (ht * TC + tc) % 2
                    vector.wait_ge(ldn[p].h, lnn[(ht, tc)])
                    vector.wait_ge(ss.h, S[("rb", tc)])
                    if ht == 0 and tc == 0:
                        vector.wait_ge(tt.h, Tt[("ff", KO - 1, TC - 1)])
                    vector.tensor_sub(cstg[p][:], xntile[p][:], mb[tc % 2][:])
                    vv.inc(vector.tensor_mul(
                        lnbuf[:, ht * TPC + tc * 512:ht * TPC + (tc + 1) * 512],
                        cstg[p][:], rb[tc % 2][:]))
            for u, (jb, mt, tc, rope) in enumerate(qkv_seq):
                for kd in range(KI):
                    gi = u * KI + kd
                    pp = gi % 2
                    vector.wait_ge(ldw[pp].h, lw[(u, kd)])
                    if gi >= 2:
                        pu, pkd = divmod(gi - 2, KI)
                        vector.wait_ge(tt.h, Tt[("qm", pu, pkd)])
                    vv.inc(vector.tensor_copy(wtb[pp][:], wtmp[pp][:]))
                if rope:
                    vector.wait_ge(tt.h, Tt[("qm", u, KI - 1)])
                    if u >= 2:
                        vector.wait_ge(stq[u % 2].h, 16 * (u // 2))
                    co = scs[:, tc * 512:(tc + 1) * 512]
                    si = ssn[:, tc * 512:(tc + 1) * 512]
                    pqs = psff[u % 2]
                    vector.tensor_mul(rt1[:], pqs[:], co)
                    vector.tensor_mul(rt2[0:64, :], pqs[64:128, :], si[0:64, :])
                    vector.tensor_mul(rt2[64:128, :], pqs[0:64, :], si[64:128, :])
                    vector.tensor_copy(vfl[:, 256:512], vfl[:, 0:256])
                    vector.tensor_sub(qstg[u % 2][0:64, :], rt1[0:64, :], rt2[0:64, :])
                    vv.inc(vector.tensor_add(qstg[u % 2][64:128, :], rt1[64:128, :], rt2[64:128, :]))

        @block.scalar
        def _(scalar):
            scalar.memzero(sfl[:])
            scalar.wait_ge(ldc.h, 16 * 8)
            for ko_ in range(KO):
                for ki in range(KI):
                    scalar.wait_ge(tt.h, Tt[("tr", ko_, ki)])
                    if ki == 0 and ko_ >= 2:
                        scalar.wait_ge(tt.h, Tt[("ff", ko_ - 2, TC - 1)])
                    cp = scalar.copy(wko[ko_ % 2][:, ki * 128:(ki + 1) * 128], psT[:])
                    if ki == KI - 1:
                        scalar.copy(sfl[:, 256:512], sfl[:, 0:256])
                        ss.inc(scalar.copy(sinkS[:, 0:1], wko[ko_ % 2][:, 0:1]))
                    else:
                        ss.inc(cp)
                for tc in range(TC):
                    p = (ko_ * TC + tc) % 2
                    scalar.wait_ge(vv.h, V[("fe", ko_, tc)])
                    gi = ko_ * TC + tc
                    if gi >= 2:
                        pko, ptc = divmod(gi - 2, TC)
                        scalar.wait_ge(tt.h, Tt[("st", pko, ptc)])
                    ss.inc(scalar.activation(sqstg[p][:], x2stg[p][:], AF.Square))
            for tc in range(TC):
                scalar.wait_ge(vv.h, V[("rows", tc)])
                o = 2048 * tc
                scalar.activation(rows[0:1, o + 1024:o + 1536], rows[0:1, o + 1536:o + 2048],
                                  AF.Ln, bias=scst[0:1, 0:1])
                scalar.copy(sfl[:, 256:512], sfl[:, 0:256])
                scalar.activation(rows[0:1, o + 1024:o + 1536], rows[0:1, o + 1024:o + 1536],
                                  AF.Exp, scale=-0.5)
                scalar.copy(sfl[:, 256:512], sfl[:, 0:256])
                ss.inc(scalar.copy(sinkS[0:1, 1:2], rows[0:1, o + 1024:o + 1025]))
                scalar.wait_ge(tt.h, Tt[("bcm", tc)])
                scalar.copy(mb[tc % 2][:], psb[:])
                scalar.copy(sfl[:, 256:512], sfl[:, 0:256])
                ss.inc(scalar.copy(sinkS[:, 2:3], mb[tc % 2][:, 0:1]))
                scalar.wait_ge(tt.h, Tt[("bcr", tc)])
                scalar.copy(rb[tc % 2][:], psb[:])
                scalar.copy(sfl[:, 256:512], sfl[:, 0:256])
                ss.inc(scalar.copy(sinkS[:, 3:4], rb[tc % 2][:, 0:1]))
            for u, (jb, mt, tc, rope) in enumerate(qkv_seq):
                if not rope:
                    scalar.wait_ge(tt.h, Tt[("qm", u, KI - 1)])
                    if u >= 2:
                        scalar.wait_ge(stq[u % 2].h, 16 * (u // 2))
                    cp = scalar.copy(qstg[u % 2][:], psff[u % 2][:])
                    scalar.copy(sfl[:, 256:512], sfl[:, 0:256])
                    ss.inc(scalar.copy(sinkS[:, 0:1], qstg[u % 2][:, 0:1]))

        @block.tensor
        def _(tensor):
            tensor.wait_ge(ldc.h, 16 * 8)
            for ko_ in range(KO):
                for ki in range(KI):
                    tensor.wait_ge(vv.h, V[("deq", ko_)])
                    if ko_ * KI + ki >= 1:
                        pk = ko_ * KI + ki - 1
                        tensor.wait_ge(ss.h, S[("tre", pk // KI, pk % KI)])
                    tt.inc(tensor.transpose(psT[:], wqb[:, ki * 128:(ki + 1) * 128], sidb[:]))
                for tc in range(TC):
                    tensor.wait_ge(ss.h, S[("tre", ko_, KI - 1)])
                    gi = ko_ * TC + tc
                    if gi >= 2:
                        pko, ptc = divmod(gi - 2, TC)
                        tensor.wait_ge(vv.h, V[("fe", pko, ptc)])
                    for ki in range(KI):
                        mm = tensor.matmul(psff[tc % 2][:],
                                           wko[ko_ % 2][:, ki * 128:(ki + 1) * 128],
                                           lnbuf[:, ki * TPC + tc * 512:ki * TPC + (tc + 1) * 512],
                                           start=(ki == 0), stop=(ki == KI - 1))
                    tt.inc(mm)
                    tensor.wait_ge(vv.h, V[("fe", ko_, tc)])
                    tensor.wait_ge(ss.h, S[("sq", ko_, tc)])
                    if ko_ == 0 and tc == 0:
                        pass
                    p = (ko_ * TC + tc) % 2
                    tensor.matmul(pss[tc][:], sonesC[:, 0:1], x2stg[p][:],
                                  start=(ko_ == 0), stop=(ko_ == KO - 1))
                    tt.inc(tensor.matmul(psq[tc][:], sonesC[:, 0:1], sqstg[p][:],
                                         start=(ko_ == 0), stop=(ko_ == KO - 1)))
            for tc in range(TC):
                tensor.wait_ge(ss.h, S[("rstd", tc)])
                if tc >= 1:
                    tensor.wait_ge(ss.h, S[("rb", tc - 1)])
                o = 2048 * tc
                tt.inc(tensor.matmul(psb[:], sonesF[:, 0:128], rows[0:1, o:o + 512],
                                     start=True, stop=True))
                tensor.wait_ge(ss.h, S[("mb", tc)])
                tt.inc(tensor.matmul(psb[:], sonesF[:, 0:128], rows[0:1, o + 1024:o + 1536],
                                     start=True, stop=True))
            for u, (jb, mt, tc, rope) in enumerate(qkv_seq):
                if u >= 2:
                    pu = u - 2
                    if qkv_seq[pu][3]:
                        tensor.wait_ge(vv.h, V[("ro", pu)])
                    else:
                        tensor.wait_ge(ss.h, S[("vc", pu)])
                elif u == 0:
                    tensor.wait_ge(vv.h, V[("ln", 31, TC - 1)])
                for kd in range(KI):
                    gk = u * KI + kd
                    tensor.wait_ge(vv.h, V[("qc", u, kd)])
                    tt.inc(tensor.matmul(psff[u % 2][:],
                                         wtb[gk % 2][:],
                                         lnbuf[:, kd * TPC + tc * 512:kd * TPC + (tc + 1) * 512],
                                         start=(kd == 0), stop=(kd == KI - 1)))

        @block.gpsimd
        def _(gpsimd):
            for ko_ in range(KO):
                for tc in range(TC):
                    gi = ko_ * TC + tc
                    p = gi % 2
                    gpsimd.wait_ge(vv.h, V[("fe", ko_, tc)])
                    stx[p].inc(gpsimd.dma_start(
                        out=x2_v[:, ko_, tc * 512:(tc + 1) * 512], in_=x2stg[p][:]), 16)
            for u, (jb, mt, tc, rope) in enumerate(qkv_seq):
                ov_ = qkv_jobs[jb][1]
                if rope:
                    gpsimd.wait_ge(vv.h, V[("ro", u)])
                else:
                    gpsimd.wait_ge(ss.h, S[("vc", u)])
                stq[u % 2].inc(gpsimd.dma_start(
                    out=ov_[:, mt, tc * 512:(tc + 1) * 512], in_=qstg[u % 2][:]), 16)
            for s_ in (stx[0], stx[1], stq[0], stq[1]):
                gpsimd.wait_ge(s_.h, s_.n)

    return nc



def build_attn():
    nc = bass.Bass("TRN2", target_bir_lowering=False, **BASS_KW)
    qi = nc.dram_tensor("qTh", [1024, L], BF16, kind="ExternalInput").ap()
    ki = nc.dram_tensor("kTh", [256, L], BF16, kind="ExternalInput").ap()
    vi = nc.dram_tensor("vTh", [256, L], BF16, kind="ExternalInput").ap()
    bandi = nc.dram_tensor("band", [128, 128], F32, kind="ExternalInput").ap()
    bandua = nc.dram_tensor("bandUa", [128, 128], F32, kind="ExternalInput").ap()
    idb = nc.dram_tensor("idb", [128, 128], BF16, kind="ExternalInput").ap()
    ao = nc.dram_tensor("aoT", [1024, L], BF16, kind="ExternalOutput").ap()
    q_v = qi.rearrange("(h p) t -> p h t", p=128)
    k_v = ki.rearrange("(h p) t -> p h t", p=128)
    v_v = vi.rearrange("(h p) t -> p h t", p=128)
    ao_v = ao.rearrange("(h p) t -> p h t", p=128)
    NH_ = 8
    NQT = L // 128
    SC = 1.0 / math.sqrt(128.0)

    units = [(h, qt) for h in range(NH_) for qt in range(NQT)]

    V, S, Tt = {}, {}, {}
    vn = sn = tn = 0
    for kv in range(2):
        for kt in range(NQT):
            tn += 1; Tt[("vtr", kv, kt)] = tn
            sn += 1; S[("vte", kv, kt)] = sn
    for u, (h, qt) in enumerate(units):
        NK = qt + 1
        NC_ = (NK * 128 + 511) // 512
        for kc in range(NC_):
            tn += 1; Tt[("qk", u, kc)] = tn
            sn += 1; S[("se", u, kc)] = sn
        vn += 1; V[("msk", u)] = vn
        sn += 1; S[("exp", u)] = sn
        vn += 1; V[("pn", u)] = vn
        for kt in range(NK):
            tn += 1; Tt[("tr", u, kt)] = tn
            sn += 1; S[("pte", u, kt)] = sn
            tn += 1; Tt[("pv", u, kt)] = tn
        sn += 1; S[("oe", u)] = sn

    with ExitStack() as ctx:
        qsb = ctx.enter_context(nc.sbuf_tensor("qsb", [128, NH_ * L], BF16))
        ksb = ctx.enter_context(nc.sbuf_tensor("ksb", [128, 2 * L], BF16))
        vsb = ctx.enter_context(nc.sbuf_tensor("vsb", [128, 2 * L], BF16))
        vtt = ctx.enter_context(nc.sbuf_tensor("vtt", [128, 2 * L], BF16))
        srow = ctx.enter_context(nc.sbuf_tensor("srow", [128, L], F32))
        prow = ctx.enter_context(nc.sbuf_tensor("prow", [128, L], BF16))
        cmx = ctx.enter_context(nc.sbuf_tensor("cmx", [128, 256], F32))
        c01 = ctx.enter_context(nc.sbuf_tensor("c01", [128, 256], F32))
        acm = ctx.enter_context(nc.sbuf_tensor("acm", [128, 256], F32))
        m8 = ctx.enter_context(nc.sbuf_tensor("m8", [128, 8], F32))
        mcol = ctx.enter_context(nc.sbuf_tensor("mcol", [128, 4], F32))
        scol = ctx.enter_context(nc.sbuf_tensor("scol", [128, 2], F32))
        mtile = ctx.enter_context(nc.sbuf_tensor("mtile", [128, 128], F32))
        sband = ctx.enter_context(nc.sbuf_tensor("sband", [128, 128], F32))
        sbandua = ctx.enter_context(nc.sbuf_tensor("sbandua", [128, 128], F32))
        sidb = ctx.enter_context(nc.sbuf_tensor("sidb", [128, 128], BF16))
        ptt = [ctx.enter_context(nc.sbuf_tensor(f"ptt{i}", [128, 128], BF16)) for i in range(2)]
        aostg = [ctx.enter_context(nc.sbuf_tensor(f"aostg{i}", [128, 128], BF16)) for i in range(2)]
        vfl = ctx.enter_context(nc.sbuf_tensor("vfl", [128, 512], F32))
        sfl = ctx.enter_context(nc.sbuf_tensor("sfl", [128, 512], F32))
        sinkV = ctx.enter_context(nc.sbuf_tensor("sinkV", [128, 4], F32))
        sinkS = ctx.enter_context(nc.sbuf_tensor("sinkS", [128, 4], F32))
        psS = [ctx.enter_context(nc.psum_tensor(f"psS{i}", [128, 512], F32)) for i in range(2)]
        psPT = [ctx.enter_context(nc.psum_tensor(f"psPT{i}", [128, 128], BF16)) for i in range(2)]
        psO = ctx.enter_context(nc.psum_tensor("psO", [128, 128], F32))
        ldc = Sem(nc, ctx, "ldc")
        vv = Sem(nc, ctx, "vv")
        ss = Sem(nc, ctx, "ss")
        tt = Sem(nc, ctx, "tt")
        sto = [Sem(nc, ctx, "sto0"), Sem(nc, ctx, "sto1")]
        block = ctx.enter_context(nc.Block())

        @block.sync
        def _(sync):
            ldc.inc(sync.dma_start(out=qsb[:].rearrange("p (h t) -> p h t", t=L), in_=q_v), 16)
            ldc.inc(sync.dma_start(out=ksb[:].rearrange("p (h t) -> p h t", t=L), in_=k_v), 16)
            ldc.inc(sync.dma_start(out=vsb[:].rearrange("p (h t) -> p h t", t=L), in_=v_v), 16)
            ldc.inc(sync.dma_start(out=sband[:], in_=bandi), 16)
            ldc.inc(sync.dma_start(out=sbandua[:], in_=bandua), 16)
            ldc.inc(sync.dma_start(out=sidb[:], in_=idb), 16)

        @block.vector
        def _(vector):
            vector.memset(vfl[:], 0.0)
            vector.wait_ge(ldc.h, 96)
            for u, (h, qt) in enumerate(units):
                NK = qt + 1
                NC_ = (NK * 128 + 511) // 512
                KW = NK * 128
                vector.wait_ge(ss.h, S[("se", u, NC_ - 1)])
                vector.tensor_add(srow[:, (NK - 1) * 128:KW], srow[:, (NK - 1) * 128:KW], sband[:])
                vector.tensor_reduce(cmx[:, 0:8 * NK],
                                     srow.ap().rearrange("p (c k) -> p c k", k=16)[:, 0:8 * NK, :],
                                     AX.X, ALU.max)
                vector.tensor_copy(vfl[:, 256:512], vfl[:, 0:256])
                vector.max(m8[:], cmx[:, 0:8 * NK])
                vector.tensor_copy(vfl[:, 256:512], vfl[:, 0:256])
                vector.tensor_scalar(c01[:, 0:8 * NK], cmx[:, 0:8 * NK], m8[:, 7:8], None, ALU.is_ge)
                vector.memset(c01[:, 0:8], 1.0)
                vector.tensor_copy(vfl[:, 256:512], vfl[:, 0:256])
                vector.tensor_scalar(acm[:, 0:8 * NK], c01[:, 0:8 * NK], 1.0, 1.0e30,
                                     ALU.subtract, ALU.mult)
                vector.tensor_scalar(mcol[:, 0:1], m8[:, 0:1], -SC, None, ALU.mult)
                vector.tensor_copy(vfl[:, 256:512], vfl[:, 0:256])
                for kt in range(1, max(1, qt - 1)):
                    vector.tensor_add(
                        srow.ap().rearrange("p (c k) -> p c k", k=16)[:, 8 * kt:8 * (kt + 1), :],
                        srow.ap().rearrange("p (c k) -> p c k", k=16)[:, 8 * kt:8 * (kt + 1), :],
                        acm[:, 8 * kt:8 * (kt + 1)].rearrange("p c -> p c 1" if False else "p (c o) -> p c o", o=1).to_broadcast([128, 8, 16]))
                if qt - 1 >= 1:
                    kt = qt - 1
                    vector.tensor_max(
                        mtile[:].rearrange("p (c k) -> p c k", k=16),
                        sbandua[:].rearrange("p (c k) -> p c k", k=16),
                        acm[:, 8 * kt:8 * (kt + 1)].rearrange("p (c o) -> p c o", o=1).to_broadcast([128, 8, 16]))
                    vector.tensor_add(srow[:, kt * 128:(kt + 1) * 128],
                                      srow[:, kt * 128:(kt + 1) * 128], mtile[:])
                vector.tensor_copy(vfl[:, 256:512], vfl[:, 0:256])
                vv.inc(vector.tensor_copy(sinkV[:, 0:1], srow[:, 0:1]))
                vector.wait_ge(ss.h, S[("exp", u)])
                vector.reciprocal(mcol[:, 1:2], scol[:, 0:1])
                vector.tensor_copy(vfl[:, 256:512], vfl[:, 0:256])
                vv.inc(vector.tensor_scalar(prow[:, 0:KW], prow[:, 0:KW], mcol[:, 1:2], None, ALU.mult))

        @block.scalar
        def _(scalar):
            scalar.memzero(sfl[:])
            scalar.wait_ge(ldc.h, 96)
            for kv in range(2):
                for kt in range(NQT):
                    scalar.wait_ge(tt.h, Tt[("vtr", kv, kt)])
                    cp = scalar.copy(vtt[:, kv * L + kt * 128:kv * L + (kt + 1) * 128], psPT[kt % 2][:])
                    if kt == NQT - 1:
                        scalar.copy(sfl[:, 256:512], sfl[:, 0:256])
                        ss.inc(scalar.copy(sinkS[:, 0:1], vtt[:, kv * L:kv * L + 1]))
                    else:
                        ss.inc(cp)
            for u, (h, qt) in enumerate(units):
                NK = qt + 1
                NC_ = (NK * 128 + 511) // 512
                KW = NK * 128
                for kc in range(NC_):
                    w = KW - kc * 512 if kc == NC_ - 1 else 512
                    w = min(512, KW - kc * 512)
                    scalar.wait_ge(tt.h, Tt[("qk", u, kc)])
                    ss.inc(scalar.copy(srow[:, kc * 512:kc * 512 + w], psS[kc % 2][:, 0:w]))
                scalar.wait_ge(vv.h, V[("msk", u)])
                if u >= 1:
                    pu = u - 1
                    scalar.wait_ge(tt.h, Tt[("tr", pu, units[pu][1])])
                scalar.activation(prow[:, 0:KW], srow[:, 0:KW], AF.Exp,
                                  bias=mcol[:, 0:1], scale=SC, accum_out=scol[:, 0:1])
                scalar.copy(sfl[:, 256:512], sfl[:, 0:256])
                ss.inc(scalar.copy(sinkS[:, 1:2], scol[:, 0:1]))
                for kt in range(NK):
                    scalar.wait_ge(tt.h, Tt[("tr", u, kt)])
                    ss.inc(scalar.copy(ptt[kt % 2][:], psPT[kt % 2][:]))
                scalar.wait_ge(tt.h, Tt[("pv", u, NK - 1)])
                if u >= 2:
                    scalar.wait_ge(sto[u % 2].h, 16 * (u // 2))
                cp = scalar.copy(aostg[u % 2][:], psO[:])
                scalar.copy(sfl[:, 256:512], sfl[:, 0:256])
                ss.inc(scalar.copy(sinkS[:, 2:3], aostg[u % 2][:, 0:1]))

        @block.tensor
        def _(tensor):
            tensor.wait_ge(ldc.h, 96)
            for kv in range(2):
                for kt in range(NQT):
                    g = kv * NQT + kt
                    if g >= 2:
                        pk = g - 2
                        tensor.wait_ge(ss.h, S[("vte", pk // NQT, pk % NQT)])
                    tt.inc(tensor.transpose(psPT[kt % 2][:],
                                            vsb[:, kv * L + kt * 128:kv * L + (kt + 1) * 128],
                                            sidb[:]))
            for u, (h, qt) in enumerate(units):
                NK = qt + 1
                NC_ = (NK * 128 + 511) // 512
                KW = NK * 128
                kvh = h // 4
                lhsq = qsb[:, h * L + qt * 128:h * L + (qt + 1) * 128]
                for kc in range(NC_):
                    w = min(512, KW - kc * 512)
                    if kc >= 2:
                        tensor.wait_ge(ss.h, S[("se", u, kc - 2)])
                    elif u > 0:
                        pu = u - 1
                        pNC = ((units[pu][1] + 1) * 128 + 511) // 512
                        lk = pNC - 1 if (pNC - 1) % 2 == kc % 2 else pNC - 2
                        if lk >= 0:
                            tensor.wait_ge(ss.h, S[("se", pu, lk)])
                    tt.inc(tensor.matmul(psS[kc % 2][:, 0:w], lhsq,
                                         ksb[:, kvh * L + kc * 512:kvh * L + kc * 512 + w],
                                         start=True, stop=True))
                tensor.wait_ge(vv.h, V[("pn", u)])
                for kt in range(NK):
                    if kt >= 2:
                        tensor.wait_ge(ss.h, S[("pte", u, kt - 2)])
                    elif u >= 1:
                        pu = u - 1
                        pNK = units[pu][1] + 1
                        lk = pNK - 1 if (pNK - 1) % 2 == kt % 2 else pNK - 2
                        if lk >= 0:
                            tensor.wait_ge(ss.h, S[("pte", pu, lk)])
                    else:
                        tensor.wait_ge(ss.h, S[("vte", 1, NQT - 2 + kt)])
                    tt.inc(tensor.transpose(psPT[kt % 2][:],
                                            prow[:, kt * 128:(kt + 1) * 128], sidb[:]))
                    tensor.wait_ge(ss.h, S[("pte", u, kt)])
                    if kt == 0 and u >= 1:
                        tensor.wait_ge(ss.h, S[("oe", u - 1)])
                    tt.inc(tensor.matmul(psO[:],
                                         vtt[:, kvh * L + kt * 128:kvh * L + (kt + 1) * 128],
                                         ptt[kt % 2][:],
                                         start=(kt == 0), stop=(kt == NK - 1)))

        @block.gpsimd
        def _(gpsimd):
            for u, (h, qt) in enumerate(units):
                gpsimd.wait_ge(ss.h, S[("oe", u)])
                sto[u % 2].inc(gpsimd.dma_start(
                    out=ao_v[:, h, qt * 128:(qt + 1) * 128], in_=aostg[u % 2][:]), 16)
            gpsimd.wait_ge(sto[0].h, sto[0].n)
            gpsimd.wait_ge(sto[1].h, sto[1].n)

    return nc



def build_ffn1():
    OP = 11264
    MO = OP // 128
    KU = 2816 // 128
    nc = bass.Bass("TRN2", target_bir_lowering=False, **BASS_KW)
    x3Ti = nc.dram_tensor("x3T", [H, TPC], F32, kind="ExternalInput").ap()
    x3lni = nc.dram_tensor("x3ln", [H, TPC], BF16, kind="ExternalInput").ap()
    w1i = nc.dram_tensor("w1qT", [1024, OP], F32, kind="ExternalInput").ap()
    b1i = nc.dram_tensor("b1T", [2, OP], F32, kind="ExternalInput").ap()
    a1i = nc.dram_tensor("a1T", [H, 2], F32, kind="ExternalInput").ap()
    w3i = nc.dram_tensor("w3qT", [1024, OP], F32, kind="ExternalInput").ap()
    b3i = nc.dram_tensor("b3T", [2, OP], F32, kind="ExternalInput").ap()
    a3i = nc.dram_tensor("a3T", [H, 2], F32, kind="ExternalInput").ap()
    w2i = nc.dram_tensor("w2qT", [2816, H], F32, kind="ExternalInput").ap()
    a2i = nc.dram_tensor("a2Tpm", [128, MO * 2], F32, kind="ExternalInput").ap()
    b2i = nc.dram_tensor("b2T", [2, H], F32, kind="ExternalInput").ap()
    idf = nc.dram_tensor("idf", [128, 128], F32, kind="ExternalInput").ap()
    oo = nc.dram_tensor("out", [TPC, H], F32, kind="ExternalOutput").ap()

    TC = TPC // 512
    x3T_v = x3Ti.rearrange("(ht p) t -> p ht t", p=128)
    x3ln_v = x3lni.rearrange("(ht p) t -> p ht t", p=128)
    w1_v = w1i.rearrange("(kd p) m -> p kd m", p=128)
    w3_v = w3i.rearrange("(kd p) m -> p kd m", p=128)
    w2_v = w2i.rearrange("(kd p) m -> p kd m", p=128)

    V, S, Tt = {}, {}, {}
    vn = sn = tn = 0
    vn += 1; V["a2c"] = vn
    vn += 1; V["fold"] = vn
    for tc in range(TC):
        tn += 1; Tt[("l1", tc)] = tn
        tn += 1; Tt[("l3", tc)] = tn
        vn += 1; V[("le", tc)] = vn
        for mo in range(MO):
            for ki in range(8):
                vn += 2; V[("wc1", tc, mo, ki)] = vn - 1; V[("wc3", tc, mo, ki)] = vn
            for ki in range(8):
                tn += 1; Tt[("gm", tc, mo, ki)] = tn
                tn += 1; Tt[("u3", tc, mo, ki)] = tn
            tn += 1; Tt[("g", tc, mo)] = tn
            tn += 1; Tt[("u", tc, mo)] = tn
            sn += 1; S[("sil", tc, mo)] = sn
            vn += 1; V[("um", tc, mo)] = vn
            tn += 1; Tt[("l2", tc, mo)] = tn
        vn += 1; V[("ufc", tc)] = vn
        vn += 1; V[("l2e", tc)] = vn
        for md in range(32):
            for ki in range(KU):
                vn += 1; V[("wc2", tc, md, ki)] = vn
            for ki in range(KU):
                tn += 1; Tt[("om", tc, md, ki)] = tn
            tn += 1; Tt[("o", tc, md)] = tn
            vn += 1; V[("oe", tc, md)] = vn
            for j in range(4):
                tn += 1; Tt[("tr", tc, md, j)] = tn
                sn += 1; S[("te", tc, md, j)] = sn

    with ExitStack() as ctx:
        lnbuf = ctx.enter_context(nc.sbuf_tensor("lnbuf", [128, 32 * TPC], BF16))
        xfb = ctx.enter_context(nc.sbuf_tensor("xfb", [128, 8 * TPC], BF16))
        sa1b = ctx.enter_context(nc.sbuf_tensor("sa1b", [128, 32 * 2], BF16))
        sa3b = ctx.enter_context(nc.sbuf_tensor("sa3b", [128, 32 * 2], BF16))
        sb1b = ctx.enter_context(nc.sbuf_tensor("sb1b", [2, OP], BF16))
        sb3b = ctx.enter_context(nc.sbuf_tensor("sb3b", [2, OP], BF16))
        sb2b = ctx.enter_context(nc.sbuf_tensor("sb2b", [2, H], BF16))
        sa2b = ctx.enter_context(nc.sbuf_tensor("sa2b", [128, MO * 2], BF16))
        sidf = ctx.enter_context(nc.sbuf_tensor("sidf", [128, 128], F32))
        l1b = ctx.enter_context(nc.sbuf_tensor("l1b", [2, 512], BF16))
        l3b = ctx.enter_context(nc.sbuf_tensor("l3b", [2, 512], BF16))
        l2b = ctx.enter_context(nc.sbuf_tensor("l2b", [2, 512], BF16))
        wt1 = [ctx.enter_context(nc.sbuf_tensor(f"wt1_{i}", [128, 128], F32)) for i in range(2)]
        wt1b = [ctx.enter_context(nc.sbuf_tensor(f"wt1b_{i}", [128, 128], BF16)) for i in range(2)]
        wt3 = [ctx.enter_context(nc.sbuf_tensor(f"wt3_{i}", [128, 128], F32)) for i in range(2)]
        wt3b = [ctx.enter_context(nc.sbuf_tensor(f"wt3b_{i}", [128, 128], BF16)) for i in range(2)]
        gstg = ctx.enter_context(nc.sbuf_tensor("gstg", [128, 512], F32))
        ustg = ctx.enter_context(nc.sbuf_tensor("ustg", [128, 512], F32))
        ubstg = [ctx.enter_context(nc.sbuf_tensor(f"ubstg{i}", [128, 512], BF16)) for i in range(2)]
        uff = ctx.enter_context(nc.sbuf_tensor("uff", [128, KU * 512], BF16))
        xtile = [ctx.enter_context(nc.sbuf_tensor(f"xtile{i}", [128, 512], F32)) for i in range(2)]
        fstg = [ctx.enter_context(nc.sbuf_tensor(f"fstg{i}", [128, 512], F32)) for i in range(2)]
        otile = [ctx.enter_context(nc.sbuf_tensor(f"otile{i}", [128, 128], F32)) for i in range(2)]
        vfl = ctx.enter_context(nc.sbuf_tensor("vfl", [128, 512], F32))
        sfl = ctx.enter_context(nc.sbuf_tensor("sfl", [128, 512], F32))
        sinkV = ctx.enter_context(nc.sbuf_tensor("sinkV", [128, 4], F32))
        sinkS = ctx.enter_context(nc.sbuf_tensor("sinkS", [128, 4], F32))
        psA = [ctx.enter_context(nc.psum_tensor(f"psA{i}", [128, 512], F32)) for i in range(2)]
        psB = [ctx.enter_context(nc.psum_tensor(f"psB{i}", [128, 512], F32)) for i in range(2)]
        psC = [ctx.enter_context(nc.psum_tensor(f"psC{i}", [2, 512], F32)) for i in range(2)]
        psT = [ctx.enter_context(nc.psum_tensor(f"psT{i}", [128, 128], F32)) for i in range(2)]
        ldc = Sem(nc, ctx, "ldc")
        ldg = Sem(nc, ctx, "ldg")
        ldw = [Sem(nc, ctx, "ldw0"), Sem(nc, ctx, "ldw1")]
        ldw3 = [Sem(nc, ctx, "ldw30"), Sem(nc, ctx, "ldw31")]
        ldr = [Sem(nc, ctx, "ldr0"), Sem(nc, ctx, "ldr1")]
        vv = Sem(nc, ctx, "vv")
        ss = Sem(nc, ctx, "ss")
        tt = Sem(nc, ctx, "tt")
        sto = [Sem(nc, ctx, "sto0"), Sem(nc, ctx, "sto1")]
        block = ctx.enter_context(nc.Block())

        lw1, lw3, lw2, lrr = {}, {}, {}, {}

        @block.sync
        def _(sync):
            ldc.inc(sync.dma_start(out=sidf[:], in_=idf), 16)
            ldc.inc(sync.dma_start(
                out=lnbuf[:].rearrange("p (ht t) -> p ht t", t=TPC), in_=x3ln_v), 16)
            for tc in range(TC):
                for mo in range(MO):
                    for ki in range(8):
                        g = len(lw1)
                        p = g % 2
                        if g >= 2:
                            pk = list(lw1.keys())[g - 2]
                            sync.wait_ge(vv.h, V[("wc1",) + pk])
                        ldw[p].inc(sync.dma_start(
                            out=wt1[p][:], in_=w1_v[:, ki, mo * 128:(mo + 1) * 128]), 16)
                        lw1[(tc, mo, ki)] = ldw[p].n
                        if g >= 2:
                            sync.wait_ge(vv.h, V[("wc3",) + pk])
                        ldw3[p].inc(sync.dma_start(
                            out=wt3[p][:], in_=w3_v[:, ki, mo * 128:(mo + 1) * 128]), 16)
                        lw3[(tc, mo, ki)] = ldw3[p].n
                for md in range(32):
                    for ki in range(KU):
                        g = len(lw2)
                        p = g % 2
                        if g >= 2:
                            pk = list(lw2.keys())[g - 2]
                            sync.wait_ge(vv.h, V[("wc2",) + pk])
                        ldw[(g % 2)].inc(sync.dma_start(
                            out=wt1[p][:], in_=w2_v[:, ki, md * 128:(md + 1) * 128]), 16)
                        lw2[(tc, md, ki)] = ldw[p].n
                    gi = tc * 32 + md
                    p = gi % 2
                    if gi >= 2:
                        pk = divmod(gi - 2, 32)
                        sync.wait_ge(vv.h, V[("oe",) + (pk[0], pk[1])])
                    ldr[p].inc(sync.dma_start(
                        out=xtile[p][:], in_=x3T_v[:, md, tc * 512:(tc + 1) * 512]), 16)
                    lrr[(tc, md)] = ldr[p].n

        @block.vector
        def _(vector):
            vector.memset(vfl[:], 0.0)
            vector.wait_ge(ldc.h, 32)
            vector.wait_ge(ldg.h, 96)
            vv.inc(vector.tensor_copy(vfl[:, 0:1], vfl[:, 1:2]))
            for kt in range(8):
                vector.tensor_add(xfb[:, kt * TPC:(kt + 1) * TPC],
                                  lnbuf[:, kt * TPC:(kt + 1) * TPC],
                                  lnbuf[:, (8 + kt) * TPC:(9 + kt) * TPC])
                vector.tensor_add(xfb[:, kt * TPC:(kt + 1) * TPC],
                                  xfb[:, kt * TPC:(kt + 1) * TPC],
                                  lnbuf[:, (16 + kt) * TPC:(17 + kt) * TPC])
                vector.tensor_add(xfb[:, kt * TPC:(kt + 1) * TPC],
                                  xfb[:, kt * TPC:(kt + 1) * TPC],
                                  lnbuf[:, (24 + kt) * TPC:(25 + kt) * TPC])
            vector.tensor_copy(vfl[:, 256:512], vfl[:, 0:256])
            vv.inc(vector.tensor_copy(sinkV[:, 0:1], xfb[:, 0:1]))
            for tc in range(TC):
                vector.wait_ge(tt.h, Tt[("l3", tc)])
                vector.tensor_scalar(l1b[:], psC[0][0:2, :], 0.5, None, ALU.mult)
                vector.tensor_scalar(l3b[:], psC[1][0:2, :], 0.5, None, ALU.mult)
                vector.tensor_copy(vfl[:, 256:512], vfl[:, 0:256])
                vv.inc(vector.tensor_copy(sinkV[0:2, 1:2], l3b[:, 0:1]))
                for mo in range(MO):
                    for ki in range(8):
                        g = tc * MO * 8 + mo * 8 + ki
                        p = g % 2
                        vector.wait_ge(ldw[p].h, lw1[(tc, mo, ki)])
                        if g >= 2:
                            pk = g - 2
                            ptc, prem = divmod(pk, MO * 8)
                            vector.wait_ge(tt.h, Tt[("gm", ptc, prem // 8, prem % 8)])
                        vv.inc(vector.tensor_copy(wt1b[p][:], wt1[p][:]))
                        vector.wait_ge(ldw3[p].h, lw3[(tc, mo, ki)])
                        if g >= 2:
                            pk = g - 2
                            ptc, prem = divmod(pk, MO * 8)
                            vector.wait_ge(tt.h, Tt[("u3", ptc, prem // 8, prem % 8)])
                        vv.inc(vector.tensor_copy(wt3b[p][:], wt3[p][:]))
                    vector.wait_ge(ss.h, S[("sil", tc, mo)])
                    vector.wait_ge(tt.h, Tt[("u", tc, mo)])
                    vector.tensor_mul(ustg[:], gstg[:], psB[mo % 2][:])
                    ko = mo % KU
                    if mo < KU:
                        vector.tensor_copy(uff[:, ko * 512:(ko + 1) * 512], ustg[:])
                    else:
                        vector.tensor_add(uff[:, ko * 512:(ko + 1) * 512],
                                          uff[:, ko * 512:(ko + 1) * 512], ustg[:])
                    if mo >= 2:
                        vector.wait_ge(tt.h, Tt[("l2", tc, mo - 2)])
                    vector.tensor_copy(vfl[:, 256:512], vfl[:, 0:256])
                    vv.inc(vector.tensor_copy(ubstg[mo % 2][:], ustg[:]))
                vector.tensor_copy(vfl[:, 256:512], vfl[:, 0:256])
                vv.inc(vector.tensor_copy(sinkV[:, 3:4], uff[:, 0:1]))
                vector.wait_ge(tt.h, Tt[("l2", tc, MO - 1)])
                vector.tensor_scalar(l2b[:], psC[0][0:2, :], 0.5, None, ALU.mult)
                vector.tensor_copy(vfl[:, 256:512], vfl[:, 0:256])
                vv.inc(vector.tensor_copy(sinkV[0:2, 2:3], l2b[:, 0:1]))
                for md in range(32):
                    for ki in range(KU):
                        g = tc * 32 * KU + md * KU + ki
                        p = g % 2
                        vector.wait_ge(ldw[p].h, lw2[(tc, md, ki)])
                        if g >= 2:
                            pk = g - 2
                            ptc, prem = divmod(pk, 32 * KU)
                            vector.wait_ge(tt.h, Tt[("om", ptc, prem // KU, prem % KU)])
                        elif tc == 0:
                            vector.wait_ge(tt.h, Tt[("gm", 0, MO - 1, 6 + g)])
                        vv.inc(vector.tensor_copy(wt1b[p][:], wt1[p][:]))
                    gi = tc * 32 + md
                    p = gi % 2
                    vector.wait_ge(tt.h, Tt[("o", tc, md)])
                    vector.wait_ge(ldr[p].h, lrr[(tc, md)])
                    if gi >= 2:
                        pk = divmod(gi - 2, 32)
                        for j in range(4):
                            vector.wait_ge(ss.h, S[("te",) + pk + (j,)])
                    vv.inc(vector.tensor_add(fstg[p][:], psA[md % 2][:], xtile[p][:]))

        @block.scalar
        def _(scalar):
            scalar.memzero(sfl[:])
            for tc in range(TC):
                for mo in range(MO):
                    scalar.wait_ge(tt.h, Tt[("g", tc, mo)])
                    scalar.activation(gstg[:], psA[mo % 2][:], AF.Silu)
                    scalar.copy(sfl[:, 256:512], sfl[:, 0:256])
                    ss.inc(scalar.copy(sinkS[:, 0:1], gstg[:, 0:1]))
                for md in range(32):
                    gi = tc * 32 + md
                    for j in range(4):
                        scalar.wait_ge(tt.h, Tt[("tr", tc, md, j)])
                        gj = gi * 4 + j
                        if gj >= 2:
                            scalar.wait_ge(sto[j % 2].h, 16 * (gj // 2))
                        cp = scalar.copy(otile[j % 2][:], psT[j % 2][:])
                        scalar.copy(sfl[:, 256:512], sfl[:, 0:256])
                        ss.inc(scalar.copy(sinkS[:, 1:2], otile[j % 2][:, 0:1]))

        @block.tensor
        def _(tensor):
            tensor.wait_ge(ldc.h, 32)
            tensor.wait_ge(ldg.h, 96)
            tensor.wait_ge(vv.h, V["fold"])
            for tc in range(TC):
                if tc >= 1:
                    tensor.wait_ge(vv.h, V[("l2e", tc - 1)])
                for kt in range(32):
                    mm1 = tensor.matmul(psC[0][0:2, :], sa1b[:, kt * 2:(kt + 1) * 2],
                                        lnbuf[:, kt * TPC + tc * 512:kt * TPC + (tc + 1) * 512],
                                        start=(kt == 0), stop=(kt == 31))
                tt.inc(mm1)
                for kt in range(32):
                    mm3 = tensor.matmul(psC[1][0:2, :], sa3b[:, kt * 2:(kt + 1) * 2],
                                        lnbuf[:, kt * TPC + tc * 512:kt * TPC + (tc + 1) * 512],
                                        start=(kt == 0), stop=(kt == 31))
                tt.inc(mm3)
                for mo in range(MO):
                    for ki in range(8):
                        g = tc * MO * 8 + mo * 8 + ki
                        tensor.wait_ge(vv.h, V[("wc1", tc, mo, ki)])
                        if ki == 0:
                            if tc * MO + mo >= 2:
                                pm = tc * MO + mo - 2
                                tensor.wait_ge(ss.h, S[("sil", pm // MO, pm % MO)])
                                tensor.wait_ge(vv.h, V[("um", pm // MO, pm % MO)])
                            if tc >= 1 and mo < 2:
                                tensor.wait_ge(vv.h, V[("oe", tc - 1, 30 + mo)])
                        tt.inc(tensor.matmul(psA[mo % 2][:], wt1b[g % 2][:],
                                      xfb[:, ki * TPC + tc * 512:ki * TPC + (tc + 1) * 512],
                                      start=(ki == 0), stop=False))
                        tensor.wait_ge(vv.h, V[("wc3", tc, mo, ki)])
                        tt.inc(tensor.matmul(psB[mo % 2][:], wt3b[g % 2][:],
                                      xfb[:, ki * TPC + tc * 512:ki * TPC + (tc + 1) * 512],
                                      start=(ki == 0), stop=False))
                    tensor.wait_ge(vv.h, V[("le", tc)])
                    tt.inc(tensor.matmul(psA[mo % 2][:], sb1b[:, mo * 128:(mo + 1) * 128],
                                         l1b[:], start=False, stop=True))
                    tt.inc(tensor.matmul(psB[mo % 2][:], sb3b[:, mo * 128:(mo + 1) * 128],
                                         l3b[:], start=False, stop=True))
                    tensor.wait_ge(vv.h, V[("um", tc, mo)])
                    tt.inc(tensor.matmul(psC[0][0:2, :], sa2b[:, mo * 2:(mo + 1) * 2],
                                         ubstg[mo % 2][:],
                                         start=(mo == 0), stop=(mo == MO - 1),
                                         skip_group_check=True))
                for md in range(32):
                    for ki in range(KU):
                        g = tc * 32 * KU + md * KU + ki
                        tensor.wait_ge(vv.h, V[("wc2", tc, md, ki)])
                        if ki == 0:
                            tensor.wait_ge(vv.h, V[("ufc", tc)])
                            gi = tc * 32 + md
                            if gi >= 2:
                                pk = divmod(gi - 2, 32)
                                tensor.wait_ge(vv.h, V[("oe",) + pk])
                        tt.inc(tensor.matmul(psA[md % 2][:], wt1b[g % 2][:],
                                      uff[:, ki * 512:(ki + 1) * 512],
                                      start=(ki == 0), stop=False))
                    tensor.wait_ge(vv.h, V[("l2e", tc)])
                    tt.inc(tensor.matmul(psA[md % 2][:], sb2b[:, md * 128:(md + 1) * 128],
                                         l2b[:], start=False, stop=True))
                    gi = tc * 32 + md
                    tensor.wait_ge(vv.h, V[("oe", tc, md)])
                    for j in range(4):
                        gj = gi * 4 + j
                        if gj >= 2:
                            pj = gj - 2
                            pgi, pj4 = divmod(pj, 4)
                            tensor.wait_ge(ss.h, S[("te", pgi // 32, pgi % 32, pj4)])
                        tt.inc(tensor.transpose(psT[j % 2][:],
                                                fstg[gi % 2][:, j * 128:(j + 1) * 128],
                                                sidf[:]))
        @block.gpsimd
        def _(gpsimd):
            ldg.inc(gpsimd.dma_start(out=sb1b[:], in_=b1i), 16)
            ldg.inc(gpsimd.dma_start(out=sb3b[:], in_=b3i), 16)
            ldg.inc(gpsimd.dma_start(out=sb2b[:], in_=b2i), 16)
            ldg.inc(gpsimd.dma_start(out=sa2b[:], in_=a2i), 16)
            ldg.inc(gpsimd.dma_start(
                out=sa1b[:].rearrange("p (ht r) -> p ht r", r=2),
                in_=a1i.rearrange("(ht p) r -> p ht r", p=128)), 16)
            ldg.inc(gpsimd.dma_start(
                out=sa3b[:].rearrange("p (ht r) -> p ht r", r=2),
                in_=a3i.rearrange("(ht p) r -> p ht r", p=128)), 16)
            for tc in range(TC):
                for md in range(32):
                    gi = tc * 32 + md
                    for j in range(4):
                        gpsimd.wait_ge(ss.h, S[("te", tc, md, j)])
                        sto[j % 2].inc(gpsimd.dma_start(
                            out=oo[tc * 512 + j * 128:tc * 512 + (j + 1) * 128,
                                   md * 128:(md + 1) * 128],
                            in_=otile[j % 2][:]), 16)
            gpsimd.wait_ge(sto[0].h, sto[0].n)
            gpsimd.wait_ge(sto[1].h, sto[1].n)

    return nc



_NC_CACHE = {}


def _get_nc(name):
    if name not in _NC_CACHE:
        _NC_CACHE[name] = {
            "ln_t": build_ln_t,
            "inproj": build_inproj_conv,
            "dtpath": build_dtpath,
            "scan": build_scan,
            "mrl_din": lambda: build_matmul_res_ln(DIN),
            "mrl_h": lambda: build_matmul_res_ln(H),
            "ffn0qkv": build_ffn0_qkv,
            "attn": build_attn,
            "ffn1": build_ffn1,
        }[name]()
    return _NC_CACHE[name]


def _pad_blocks(a, blk=2752, pad=2816):
    nb = a.shape[0] // blk
    out = np.zeros((nb * pad, a.shape[1]), a.dtype)
    for j in range(nb):
        out[j * pad:j * pad + blk] = a[j * blk:(j + 1) * blk]
    return out


def _cc(a):
    return np.ascontiguousarray(a)


def kernel(**inp):
    cores = list(range(NCORES))
    f32 = np.float32
    bf = BF

    x = np.asarray(inp["L0_attn_hidden_states"], f32).reshape(T, H)
    cst_ln = np.zeros((128, 4), f32); cst_ln[:, 0] = EPS_LN
    cst_rms = np.zeros((128, 4), f32); cst_rms[:, 0] = EPS_RMS; cst_rms[:, 1] = 1.0
    onesF = np.ones((1, 128), f32)
    onesC = np.ones((128, 1), f32)

    ims = [dict(x=_cc(x[c * TPC:(c + 1) * TPC]), cst=cst_ln,
                idf=IDENT_F32, idb=IDENT_BF16) for c in cores]
    r1 = run_bass_kernel_spmd(_get_nc("ln_t"), ims, core_ids=cores).results
    xlnT_full = _cc(np.concatenate([np.asarray(r["xlnT"]) for r in r1], axis=1))
    xT_c = [np.asarray(r["xT"]) for r in r1]

    Wfull = np.asarray(inp["L0_attn_in_proj_w"], f32)
    xproj = np.asarray(inp["L0_attn_x_proj_w"], f32)
    convw = np.asarray(inp["L0_attn_conv_w"], f32)
    convb = np.asarray(inp["L0_attn_conv_b"], f32)
    ims = []
    for c in cores:
        dsl = slice(c * DPC, (c + 1) * DPC)
        ims.append(dict(
            xlnT=xlnT_full,
            Wx=_cc(Wfull[:, dsl]),
            Wz=_cc(Wfull[:, DIN + c * DPC:DIN + (c + 1) * DPC]),
            xprojT=_cc(xproj[:, dsl].T),
            convw=_cc(convw[dsl]),
            convb=_cc(convb[dsl].reshape(DPC // 128, 128).T)))
    r2 = run_bass_kernel_spmd(_get_nc("inproj"), ims, core_ids=cores).results
    ssmp_all = _cc(np.stack([np.asarray(r["ssmp"]) for r in r2], axis=0))
    xc_c = [np.asarray(r["xc"]) for r in r2]
    z_c = [np.asarray(r["z"]) for r in r2]

    lnw = np.concatenate([np.asarray(inp["L0_attn_dt_ln_w"], f32),
                          np.asarray(inp["L0_attn_B_ln_w"], f32),
                          np.asarray(inp["L0_attn_C_ln_w"], f32)]).reshape(1, 288)
    dtproj = np.asarray(inp["L0_attn_dt_proj_w"], f32)
    dt_bias = np.asarray(inp["L0_attn_dt_bias"], f32)
    ims = []
    for c in cores:
        dsl = slice(c * DPC, (c + 1) * DPC)
        ims.append(dict(
            ssmp=ssmp_all, lnw=_cc(lnw),
            dtprojT=_cc(dtproj[dsl].T),
            dtb=_cc(dt_bias[dsl].reshape(DPC // 128, 128).T),
            onesF=onesF, idb=IDENT_BF16, idf=IDENT_F32, cst=cst_rms))
    r3 = run_bass_kernel_spmd(_get_nc("dtpath"), ims, core_ids=cores).results
    BC = np.asarray(r3[0]["BC"])
    dt_c = [np.asarray(r["dt"]) for r in r3]

    AL = np.asarray(inp["L0_attn_A_log"], f32)
    Dp = np.asarray(inp["L0_attn_D_param"], f32)
    ims = []
    for c in cores:
        dsl = slice(c * DPC, (c + 1) * DPC)
        ims.append(dict(
            xc=xc_c[c], z=z_c[c], dt=dt_c[c], BC=BC,
            ALpm=_cc(AL[dsl].reshape(8, 128, 16).transpose(1, 0, 2).reshape(128, 128)),
            Dpm=_cc(Dp[dsl].reshape(8, 128).T), onesF=onesF))
    r4 = run_bass_kernel_spmd(_get_nc("scan"), ims, core_ids=cores).results
    y2_full = np.concatenate([np.asarray(r["y2"]) for r in r4], axis=0)

    opT = _cc(np.asarray(inp["L0_attn_out_proj_w"], f32).T)
    ims = []
    for c in cores:
        tsl = slice(c * TPC, (c + 1) * TPC)
        ims.append(dict(act=_cc(y2_full[:, tsl]), xT=xT_c[c], wT=opT,
                        onesF=onesF, onesC=onesC, cst=cst_ln))
    r5a = run_bass_kernel_spmd(_get_nc("mrl_din"), ims, core_ids=cores).results
    x1T_c = [np.asarray(r["x1T"]) for r in r5a]
    x1ln_c = [np.asarray(r["x1ln"]) for r in r5a]

    qw = _cc(np.asarray(inp["L0_ffn_qweight"]).reshape(H, H))
    scales = _cc(np.asarray(inp["L0_ffn_scales"], f32).reshape(H, 32))
    zeros = _cc(np.asarray(inp["L0_ffn_scaled_zeros"], f32).reshape(H, 32))
    bias0 = _cc(np.asarray(inp["L0_ffn_bias"], f32).reshape(32, 128).T)
    wqT = _cc(np.asarray(inp["L1_attn_wq"], f32).T)
    wkT = _cc(np.asarray(inp["L1_attn_wk"], f32).T)
    wvT = _cc(np.asarray(inp["L1_attn_wv"], f32).T)
    inv_freq = 1.0 / 10000.0 ** (np.arange(0, HD, 2) / HD)
    freqs = np.arange(L)[:, None] * inv_freq[None, :]
    emb = np.concatenate([freqs, freqs], axis=-1)
    cosT_full = np.cos(emb).T.astype(f32)
    sinT_full = np.sin(emb).T.astype(f32)
    ims = []
    for c in cores:
        lsl = slice((c % 4) * TPC, (c % 4 + 1) * TPC)
        ims.append(dict(
            x1T=x1T_c[c], x1ln=x1ln_c[c], qw=qw, scales=scales, zeros=zeros,
            bias0=bias0, wqT=wqT, wkT=wkT, wvT=wvT,
            cosT=_cc(cosT_full[:, lsl]), sinT=_cc(sinT_full[:, lsl]),
            idb=IDENT_BF16, onesF=onesF, onesC=onesC, cst=cst_ln))
    r5b = run_bass_kernel_spmd(_get_nc("ffn0qkv"), ims, core_ids=cores).results
    x2T_c = [np.asarray(r["x2T"]) for r in r5b]
    q_full = [np.concatenate([np.asarray(r5b[4 * b + g]["qT"]) for g in range(4)], axis=1)
              for b in range(B)]
    k_full = [np.concatenate([np.asarray(r5b[4 * b + g]["kT"]) for g in range(4)], axis=1)
              for b in range(B)]
    v_full = [np.concatenate([np.asarray(r5b[4 * b + g]["vT"]) for g in range(4)], axis=1)
              for b in range(B)]

    band = np.where(np.arange(128)[None, :] <= np.arange(128)[:, None], 0.0, -1.0e30).astype(f32)
    bandUa = np.where(np.arange(128)[None, :] > np.arange(128)[:, None], 0.0, -1.0e30).astype(f32)
    ims = []
    for c in cores:
        b = c // 4; g = c % 4
        ims.append(dict(
            qTh=_cc(q_full[b][1024 * g:1024 * (g + 1)]),
            kTh=_cc(k_full[b][256 * g:256 * (g + 1)]),
            vTh=_cc(v_full[b][256 * g:256 * (g + 1)]),
            band=band, bandUa=bandUa, idb=IDENT_BF16))
    r6 = run_bass_kernel_spmd(_get_nc("attn"), ims, core_ids=cores).results
    ao_full = [np.concatenate([np.asarray(r6[4 * b + g]["aoT"]) for g in range(4)], axis=0)
               for b in range(B)]

    woT = _cc(np.asarray(inp["L1_attn_wo"], f32).T)
    ims = []
    for c in cores:
        b = c // 4; g = c % 4
        ims.append(dict(
            act=_cc(ao_full[b][:, TPC * g:TPC * (g + 1)]),
            xT=x2T_c[c], wT=woT, onesF=onesF, onesC=onesC, cst=cst_ln))
    r7a = run_bass_kernel_spmd(_get_nc("mrl_h"), ims, core_ids=cores).results
    x3T_c = [np.asarray(r["x1T"]) for r in r7a]
    x3ln_c = [np.asarray(r["x1ln"]) for r in r7a]

    w1qT = _cc(_pad_blocks(np.asarray(inp["L1_ffn_w1q"], f32)).T)
    w3qT = _cc(_pad_blocks(np.asarray(inp["L1_ffn_w3q"], f32)).T)
    b1T = _cc(_pad_blocks(np.asarray(inp["L1_ffn_b1"], f32)).T)
    b3T = _cc(_pad_blocks(np.asarray(inp["L1_ffn_b3"], f32)).T)
    w2qT = np.zeros((2816, H), f32)
    w2qT[:2752] = np.asarray(inp["L1_ffn_w2q"], f32).T
    a1T = _cc(np.asarray(inp["L1_ffn_a1"], f32).T)
    a3T = _cc(np.asarray(inp["L1_ffn_a3"], f32).T)
    a2p = _pad_blocks(np.asarray(inp["L1_ffn_a2"], f32).T)
    a2Tpm = _cc(a2p.reshape(88, 128, 2).transpose(1, 0, 2).reshape(128, 176))
    b2T = _cc(np.asarray(inp["L1_ffn_b2"], f32).T)
    ims = []
    for c in cores:
        ims.append(dict(
            x3T=x3T_c[c], x3ln=x3ln_c[c],
            w1qT=w1qT, b1T=b1T, a1T=a1T, w3qT=w3qT, b3T=b3T, a3T=a3T,
            w2qT=w2qT, a2Tpm=a2Tpm, b2T=b2T, idf=IDENT_F32))
    r7b = run_bass_kernel_spmd(_get_nc("ffn1"), ims, core_ids=cores).results
    out = np.concatenate([np.asarray(r["out"]) for r in r7b], axis=0)
    return out.reshape(B, L, H).astype(np.float32)
